# version 19
# baseline (speedup 1.0000x reference)
import os
import numpy as np
import ml_dtypes
import concourse.bass as bass
import concourse.mybir as mybir
from concourse.bass_utils import run_bass_kernel_spmd

F32 = mybir.dt.float32
BF16 = mybir.dt.bfloat16
U8 = mybir.dt.uint8
AF = mybir.ActivationFunctionType
ALU = mybir.AluOpType
AXX = mybir.AxisListType.X

NC = 8
SEQS = 2
S = 2048
D = 2048
H = 16
HD = 128
NT = 16
DC = 16
EPS = 1e-5
NE, RK = 8, 64
ER = NE * RK
ERC = 4
OC = 4
DEN_SC = 1.0

ENGINES = ("sync", "scalar", "vector", "tensor", "gpsimd")


class KB:
    def __init__(self, nc):
        self.nc = nc
        self.ops = {e: [] for e in ENGINES}
        self.sems = {}
        self.cnt = {}
        self._ctx = []

    def sem(self, name):
        if name not in self.sems:
            cm = self.nc.semaphore(name)
            self.sems[name] = cm.__enter__()
            self._ctx.append(cm)
            self.cnt[name] = 0
        return name

    def emit(self, eng, fn, waits=(), inc=None, inc_n=1):
        ws = tuple((s, v) for (s, v) in waits if v > 0)
        self.ops[eng].append((fn, ws, inc, inc_n))
        if inc is not None:
            self.cnt[inc] += inc_n
            return self.cnt[inc]
        return None

    def dma(self, eng, out, in_, waits=(), inc=None):
        def fn(e, out=out, in_=in_):
            return e.dma_start(out=out, in_=in_)
        return self.emit(eng, fn, waits, inc, 16)

    def build(self):
        with self.nc.Block() as blk:
            for eng in ENGINES:
                getattr(blk, eng)(self._mk(eng))

    def _mk(self, eng):
        ops = self.ops[eng]
        sems = self.sems

        def body(e):
            seen = {}
            for fn, waits, inc, inc_n in ops:
                for s_, v_ in waits:
                    if seen.get(s_, 0) < v_:
                        e.wait_ge(sems[s_], v_)
                        seen[s_] = v_
                inst = fn(e)
                if inc is not None:
                    inst.then_inc(sems[inc], inc_n)
        return body


def build_program(n_heads=H, do_ph5=True, n_seqs=SEQS, do_attn=True, head_stage=4, n_iters=8):
    nc = bass.Bass("TRN2", target_bir_lowering=False, detect_race_conditions=False)

    x_in = nc.dram_tensor("x", [SEQS * S, D], F32, kind="ExternalInput").ap()
    y_out = nc.dram_tensor("y", [SEQS * S, D], F32, kind="ExternalOutput").ap()
    wqkv_in = nc.dram_tensor("wqkv", [H, DC, 128, 384], F32, kind="ExternalInput").ap()
    wout_in = nc.dram_tensor("wout", [H, 128, D], F32, kind="ExternalInput").ap()
    ffnA_in = nc.dram_tensor("ffnA", [128, DC * ER], F32, kind="ExternalInput").ap()
    wr_in = nc.dram_tensor("wr", [128, DC * NE], F32, kind="ExternalInput").ap()
    ffnB_in = nc.dram_tensor("ffnB", [128, ERC * D], F32, kind="ExternalInput").ap()
    ident_in = nc.dram_tensor("ident", [128, 128], BF16, kind="ExternalInput").ap()
    ctok_in = nc.dram_tensor("ctok", [128, NT * HD], F32, kind="ExternalInput").ap()
    ssg_in = nc.dram_tensor("ssg", [128, NT * HD], F32, kind="ExternalInput").ap()
    maskw_in = nc.dram_tensor("maskw", [128, 896], BF16, kind="ExternalInput").ap()
    gcol_in = nc.dram_tensor("gcolv", [128, 1], F32, kind="ExternalInput").ap()
    bcol_in = nc.dram_tensor("bcolv", [128, 1], F32, kind="ExternalInput").ap()
    br_in = nc.dram_tensor("brx", [128, 8], F32, kind="ExternalInput").ap()
    onesb_in = nc.dram_tensor("onesb", [128, 1], BF16, kind="ExternalInput").ap()
    eps_in = nc.dram_tensor("epsc", [128, 1], F32, kind="ExternalInput").ap()
    nh_in = nc.dram_tensor("nhc", [128, 1], F32, kind="ExternalInput").ap()
    scv_in = nc.dram_tensor("scv", [1, 128], F32, kind="ExternalInput").ap()

    wq_scr = nc.dram_tensor("wq_scr", [H, 128, DC * 384], BF16, kind="Internal").ap()
    wo_scr = nc.dram_tensor("wo_scr", [H, 128, D], BF16, kind="Internal").ap()
    fa_scr = nc.dram_tensor("fa_scr", [128, DC * ER], BF16, kind="Internal").ap()
    wr_scr = nc.dram_tensor("wr_scr", [128, DC * NE], BF16, kind="Internal").ap()
    fb_scr = nc.dram_tensor("fb_scr", [128, ERC * D], BF16, kind="Internal").ap()

    import contextlib
    es = contextlib.ExitStack()

    def sbuf(name, shape, dt):
        return es.enter_context(nc.sbuf_tensor(name, shape, dt))

    u_wout = sbuf("u_wout", [128, DC * S], BF16)
    ao = sbuf("ao", [128, H * S], BF16)
    ident = sbuf("identb", [128, 128], BF16)
    gcol = sbuf("gcol", [128, 1], F32)
    bcol = sbuf("bcol", [128, 1], F32)
    onesb = sbuf("onesbf", [128, 1], BF16)
    scvec = sbuf("scvec", [1, 128], F32)
    maskw = sbuf("maskwb", [128, 896], BF16)
    brx = sbuf("brxb", [128, 8], F32)
    epsc = sbuf("epscb", [128, 1], F32)
    nhc = sbuf("nhcb", [128, 1], F32)
    stat = sbuf("stat", [128, 384], F32)
    arena = sbuf("arena", [128, 76800], U8)

    psA = es.enter_context(nc.psum_tensor("psA", [128, 1024], F32))
    psT = es.enter_context(nc.psum_tensor("psT", [128, 1024], F32))
    psS = es.enter_context(nc.psum_tensor("psS", [128, 1024], F32))
    psN = es.enter_context(nc.psum_tensor("psN", [128, 512], F32))
    psD = es.enter_context(nc.psum_tensor("psD", [128, 512], F32))
    psTb = psT[:].bitcast(BF16)

    def av(off, nbytes, dt):
        return arena[:, off:off + nbytes].bitcast(dt)

    x_t = [av(0, 8192, F32), av(8192, 8192, F32)]
    sqscr = av(16384, 8192, F32)
    u_tm = [av(33792, 4096, BF16), av(37888, 4096, BF16)]
    wstage = [av(43008, 6144, F32), av(49152, 6144, F32)]
    wcvt = [av(55296, 3072, BF16), av(58368, 3072, BF16)]

    wbf = [av(0, 12288, BF16), av(12288, 12288, BF16)]
    ctok = av(24576, 8192, F32)
    ssg = av(32768, 8192, F32)
    qfm = [av(40960, 4096, BF16), av(68608, 4096, BF16)]
    kfm = [av(45056, 4096, BF16), av(72704, 4096, BF16)]
    vtm = av(49152, 4096, BF16)
    vx = [av(54272, 1024, BF16), av(56320, 1024, BF16)]
    qr = av(53248, 1024, BF16)
    tmpc = av(55296, 1024, BF16)
    p_b = [av(57344, 1024, BF16), av(58368, 1024, BF16)]
    tmps = av(63488, 1024, BF16)
    sq_b = [av(64512, 2048, F32), av(66560, 2048, F32)]
    tdiag = av(59392, 2048, F32)
    rd_full = av(59392, 2048, F32)
    rd_b = rd_full[0:1, :]
    num_sb = av(61440, 1024, BF16)

    ffnA = av(0, 16384, BF16)
    wrA = av(16384, 256, BF16)
    ffnB = av(16640, 16384, BF16)
    r1 = av(33024, 8192, F32)
    xst = [av(41216, 4096, F32), av(45312, 4096, F32)]
    w_tm = av(49408, 4096, BF16)
    w_fm = av(53504, 4096, BF16)
    z_tm = av(57600, 1024, BF16)
    z_fm = av(58624, 1024, BF16)
    logit_sb = av(59648, 32, F32)
    mask8 = av(59680, 16, BF16)
    mask_exp = av(59712, 1024, BF16)
    out_sb = [av(68608, 2048, F32), av(74752, 2048, F32)]
    xstB = [av(60736, 4096, F32), av(70656, 4096, F32)]

    SLOT = {}
    for i, nm in enumerate(["xsum0", "xsum1", "var0", "var1", "rstd0", "rstd1",
                            "mr0", "mr1", "lntmp", "negm", "ssum", "w64",
                            "tmp0", "tmp1", "tmp2", "tmp3"]):
        SLOT[nm] = stat[:, i * 16:i * 16 + 1]
    q4 = stat[:, 256:260]
    qs4 = stat[:, 272:276]
    m4 = stat[:, 288:292]
    v4 = stat[:, 304:308]
    rstd4 = stat[:, 320:324]
    mr4 = stat[:, 336:340]
    esb8 = stat[:, 352:360]
    m4sq = stat[:, 360:364]

    K = KB(nc)
    for s_ in ["t", "v", "s", "xa", "wl", "so", "sq", "vs"]:
        K.sem(s_)
    C = K.cnt

    def t_op(fn, waits=(), inc="t"):
        return K.emit("tensor", fn, waits, inc)

    def v_op(fn, waits=(), inc="v"):
        return K.emit("vector", fn, waits, inc)

    def s_op(fn, waits=(), inc="s"):
        return K.emit("scalar", fn, waits, inc)

    def vsync():
        m = K.emit("vector", lambda e: e.engine_nop(), (), "vs", 1)
        K.emit("vector", lambda e: e.engine_nop(), [("vs", m)], None)

    u3 = u_wout[:].rearrange("p (dc s) -> p dc s", dc=DC)
    uh = u_wout[:].rearrange("p (h f) -> p h f", h=H)
    ct3 = ctok.rearrange("p (nt f) -> p nt f", nt=NT)
    sg3 = ssg.rearrange("p (nt f) -> p nt f", nt=NT)
    psA4 = psA[:].rearrange("p (t g f) -> p g t f", t=2, g=8)
    psA4 = psA[:].rearrange("p (t g f) -> p g t f", t=2, g=4)
    qk_ps = psA4[:, 0:2]
    v_ps = psA4[:, 2]

    chunks = []
    for h in range(H):
        for g in range(4):
            chunks.append((wqkv_in[h, 4 * g:4 * (g + 1)].rearrange("c p f -> p c f"),
                           wq_scr[h][:, g * 1536:(g + 1) * 1536], 1536))
    for h in range(H):
        for g in range(2):
            chunks.append((wout_in[h][:, g * 1024:(g + 1) * 1024],
                           wo_scr[h][:, g * 1024:(g + 1) * 1024], 1024))
    for g in range(8):
        chunks.append((ffnA_in[:, g * 1024:(g + 1) * 1024],
                       fa_scr[:, g * 1024:(g + 1) * 1024], 1024))
    chunks.append((wr_in[:, 0:128], wr_scr[:, 0:128], 128))
    for g in range(8):
        chunks.append((ffnB_in[:, g * 1024:(g + 1) * 1024],
                       fb_scr[:, g * 1024:(g + 1) * 1024], 1024))

    K.dma("sync", ident[:], ident_in, inc="wl")
    K.dma("sync", gcol[:], gcol_in, inc="wl")
    K.dma("sync", bcol[:], bcol_in, inc="wl")
    K.dma("sync", onesb[:], onesb_in, inc="wl")
    K.dma("sync", scvec[:], scv_in, inc="wl")
    K.dma("sync", maskw[:], maskw_in, inc="wl")
    K.dma("sync", brx[:], br_in, inc="wl")
    K.dma("sync", epsc[:], eps_in, inc="wl")
    K.dma("sync", nhc[:], nh_in, inc="wl")
    wl_consts = C["wl"]

    ph0_marks = {}
    ph0_cv = {}

    def sched_ph0_chunk(i):
        src, dst, ncols = chunks[i]
        par = i % 2
        stg = wstage[par][:, 0:ncols]
        cvt = wcvt[par][:, 0:ncols]
        waits = []
        if i >= 2:
            waits.append(ph0_cv[i - 2])
        ldm = K.dma("sync", stg, src, waits, inc="wl")
        cwaits = [("wl", ldm)]
        if i >= 2:
            cwaits.append(("so", ph0_marks[i - 2]))
        if i % 2 == 0:
            cm = v_op(lambda e, o=cvt, s2=stg: e.tensor_copy(o, s2), cwaits)
            ph0_cv[i] = ("v", cm)
        else:
            cm = s_op(lambda e, o=cvt, s2=stg: e.copy(o, s2), cwaits)
            ph0_cv[i] = ("s", cm)
        ph0_marks[i] = K.dma("gpsimd", dst, cvt, [ph0_cv[i]], inc="so")

    def ph0_head_ready(h):
        return ph0_marks[4 * h + 3]

    last_qkv_mm = [0]
    attn_last = {"t": 0, "v": 0, "s": 0}
    ph5_last = {"t": 0, "v": 0, "s": 0, "so": 0}
    zfm_s = [0]

    def sched_ph1(seq, interleave_ph0, entry_t=0):
        ph0_i = [0]

        def maybe_ph0(k=2):
            for _ in range(k):
                if interleave_ph0 and ph0_i[0] < len(chunks):
                    sched_ph0_chunk(ph0_i[0])
                    ph0_i[0] += 1

        apply_mark = {}
        trans_mark = {}
        evac_mark = {}
        red_mark = {}
        for nt in range(NT):
            par = nt % 2
            maybe_ph0()
            waits = [("t", entry_t)] if nt < 2 else \
                [("s", apply_mark[nt - 2]), ("v", red_mark[nt - 2])]
            xm = K.dma("gpsimd", x_t[par],
                       x_in[seq * S + nt * 128: seq * S + (nt + 1) * 128, :], waits, inc="xa")
            rm = v_op(lambda e, x=x_t[par], o=SLOT["xsum" + str(par)]:
                      e.reduce_sum(o, x, axis=AXX), [("xa", xm)])
            red_mark[nt] = rm
            sw = [("xa", xm)]
            if nt < 2 and entry_t:
                sw.append(("t", entry_t))
            sm = s_op(lambda e, x=x_t[par], a=SLOT["var" + str(par)]:
                      e.activation(sqscr, x, AF.Square, bias=0.0, scale=1.0,
                                   accum_out=a[:, 0:1]), sw)
            mslot = SLOT["tmp0"]
            v_op(lambda e, i=SLOT["xsum" + str(par)], o=mslot:
                 e.tensor_scalar_mul(o, i, 1.0 / D), [("s", sm)], inc=None)
            v_op(lambda e, i=SLOT["var" + str(par)], o=SLOT["tmp1"]:
                 e.tensor_scalar_mul(o, i, 1.0 / D), inc=None)
            vsync()
            v_op(lambda e, m=mslot, o=SLOT["tmp2"]: e.tensor_mul(o, m, m), inc=None)
            vsync()
            vv = v_op(lambda e, o=SLOT["var" + str(par)]:
                      e.tensor_sub(o, SLOT["tmp1"], SLOT["tmp2"]))
            l1 = s_op(lambda e, i=SLOT["var" + str(par)]:
                      e.activation(SLOT["lntmp"], i, AF.Ln, bias=epsc[:, 0:1],
                                   scale=1.0), [("v", vv)])
            e1 = s_op(lambda e, o=SLOT["rstd" + str(par)]:
                      e.activation(o, SLOT["lntmp"], AF.Exp, bias=0.0,
                                   scale=nhc[:, 0:1]), [("s", l1)])
            nm = v_op(lambda e, m=mslot, r=SLOT["rstd" + str(par)],
                      o=SLOT["mr" + str(par)]:
                      e.tensor_scalar(o, m, r[:, 0:1], -1.0, op0=ALU.mult,
                                      op1=ALU.mult), [("s", e1)])
            aw = [("v", nm)]
            if nt >= 2:
                aw.append(("t", trans_mark[nt - 2]))
            am = s_op(lambda e, x=x_t[par], o=u_tm[par], r=SLOT["rstd" + str(par)],
                      b=SLOT["mr" + str(par)]:
                      e.activation(o, x, AF.Identity, bias=b[:, 0:1], scale=r[:, 0:1]), aw)
            apply_mark[nt] = am
            maybe_ph0()
            for g in range(4):
                half = g % 2
                gw = [("s", am), ("wl", wl_consts)]
                prev = (nt, g - 2) if g >= 2 else (nt - 1, g + 2)
                if prev in evac_mark:
                    gw.append(("v", evac_mark[prev]))
                tm = None
                for j in range(4):
                    dc = 4 * g + j
                    tm = t_op(lambda e, i=u_tm[par][:, dc * 128:(dc + 1) * 128],
                              o=psTb[:, half * 1024 + j * 128: half * 1024 + (j + 1) * 128]:
                              e.transpose(o, i, ident[:]), gw if j == 0 else (),
                              inc="t" if j == 3 else None)
                trans_mark[nt] = tm
                em = v_op(lambda e, o=u3[:, 4 * g:4 * (g + 1), nt * 128:(nt + 1) * 128],
                          i=psTb[:, half * 1024: half * 1024 + 512].rearrange(
                              "p (a b) -> p a b", a=4):
                          e.tensor_copy(o, i), [("t", tm)])
                evac_mark[(nt, g)] = em
        while interleave_ph0 and ph0_i[0] < len(chunks):
            sched_ph0_chunk(ph0_i[0])
            ph0_i[0] += 1
        return evac_mark[(NT - 1, 3)]

    def sched_heads(seq, u_ready_v, extra_entry=()):
        entry = [("v", u_ready_v)] + list(extra_entry)
        cm1 = K.dma("sync", ctok[:], ctok_in, entry, inc="wl")
        K.dma("sync", ssg[:], ssg_in, inc="wl")
        tables_wl = C["wl"]

        wb_mark = {0: K.dma("sync", wbf[0][:], wq_scr[0],
                            entry + [("so", ph0_head_ready(0))], inc="wl")}
        prev_qkv_w_reader = {}
        num_evac_s = attn_last["s"]
        ao_mult_v = attn_last["v"]
        bcm = attn_last["t"]
        last_ao_v = 0

        ast_ = {"nem": num_evac_s, "aom": ao_mult_v, "bcm": bcm, "kti": 0,
                "er": 0, "rcm": 0}
        sq_mark = {}
        p_mark = {}
        dn_mark = {}
        pv_mark = {}
        ropeg = {}
        vevg = {}
        sqmg = {}
        trmg = {}
        gbeg = {}
        pend_tr = [None]
        pend_qc = []
        ktq = []

        def drain(n):
            while n > 0 and ktq:
                fn, kt = ktq.pop(0)
                fn(kt)
                n -= 1

        for h in range(n_heads):
            par = h % 2
            if h + 1 < n_heads:
                w = [("so", ph0_head_ready(h + 1))]
                if h >= 1:
                    w.append(("t", prev_qkv_w_reader[h - 1]))
                else:
                    w += entry
                wb_mark[h + 1] = K.dma("sync", wbf[(h + 1) % 2][:], wq_scr[h + 1], w,
                                       inc="wl")

            it_marks = {}

            def emit_tr(pit, it_marks=it_marks, par=par, h=h):
                pnt = 2 * pit
                g = h * n_iters + pit
                halfT = pit % 2
                apm = it_marks[(pit, "apm")]
                trm = None
                for j in range(4):
                    w2 = [("v", apm)]
                    if g - 2 in gbeg:
                        w2.append(("s", gbeg[g - 2]))
                    trm = t_op(lambda e, i=tmps[:, j * 128:(j + 1) * 128],
                               o=psTb[:, halfT * 1024 + j * 128:
                                      halfT * 1024 + (j + 1) * 128]:
                               e.transpose(o, i, ident[:]), w2 if j == 0 else (),
                               inc="t" if j == 3 else None)
                it_marks[(pit, "trm")] = trm
                trmg[g] = trm
                s_op(lambda e, o=qfm[par][:, pnt * 128:(pnt + 2) * 128],
                     i=psTb[:, halfT * 1024: halfT * 1024 + 256]:
                     e.activation(o, i, AF.Identity, bias=bcol[:, 0:1],
                                  scale=gcol[:, 0:1]), [("t", trm)], inc=None)
                gb2 = s_op(lambda e, o=kfm[par][:, pnt * 128:(pnt + 2) * 128],
                           i=psTb[:, halfT * 1024 + 256: halfT * 1024 + 512]:
                           e.activation(o, i, AF.Identity, bias=bcol[:, 0:1],
                                        scale=gcol[:, 0:1]))
                it_marks[(pit, "gbe")] = gb2
                gbeg[g] = gb2

            def sched_qc(qc, it_marks=it_marks, par=par, h=h):
                qgate = it_marks[(2 * qc + 1, "gbe")]
                ktn = 4 * qc + 4
                qsl = slice(qc * 512, (qc + 1) * 512)

                def emit_pv(kt, kti, it_marks=it_marks, par=par, h=h):
                    wt = [("v", p_mark[kti]), ("s", it_marks[(kt // 2, "vev")])]
                    if kt == 0:
                        wt.append(("s", ast_["nem"]))
                        wt.append(("v", ast_["aom"]))
                    vsrc = vx[par] if kt < 4 else vtm
                    pvm = t_op(lambda e, l=vsrc[:, kt * 128:(kt + 1) * 128],
                               r=p_b[kti % 2], o=psN[:],
                               st=(kt == 0), sp=(kt == ktn - 1):
                               e.matmul(o, l, r, start=st, stop=sp), wt)
                    dnm = t_op(lambda e, r=p_b[kti % 2], o=psD[0:1, :],
                               st=(kt == 0), sp=(kt == ktn - 1):
                               e.matmul(o, onesb[:], r, start=st, stop=sp))
                    pv_mark[kti] = pvm
                    dn_mark[kti] = dnm

                def unit(kt, qgate=qgate, ktn=ktn, qsl=qsl, qc=qc,
                         it_marks=it_marks, par=par, h=h, emit_pv=emit_pv):
                    kti = ast_["kti"]
                    w = [("s", qgate)]
                    if kti >= 2:
                        w.append(("s", sq_mark[kti - 2]))
                    smm = t_op(lambda e, l=kfm[par][:, kt * 128:(kt + 1) * 128],
                               o=psS[:, (kti % 2) * 512:(kti % 2) * 512 + 512],
                               r=qfm[par][:, qsl]:
                               e.matmul(o, l, r, start=True, stop=True), w)
                    if kt >= 1:
                        emit_pv(kt - 1, kti - 1)
                    ws = [("t", smm)]
                    if kti >= 2:
                        ws.append(("v", p_mark[kti - 2]))
                    if kt == 0 and ast_.get("rcm"):
                        ws.append(("v", ast_["rcm"]))
                    sqm2 = s_op(lambda e,
                                i=psS[:, (kti % 2) * 512:(kti % 2) * 512 + 512],
                                o=sq_b[kti % 2]:
                                e.activation(o, i, AF.Square, bias=0.0, scale=1.0),
                                ws)
                    sq_mark[kti] = sqm2
                    wv = [("s", sqm2)]
                    if kti >= 2:
                        wv.append(("t", dn_mark[kti - 2]))
                    if kt >= 4 * qc:
                        rel = kt * 128 - qc * 512
                        msl = maskw[:, 384 - rel: 896 - rel]
                        if kt == 4 * qc and ast_.get("er"):
                            wv.append(("s", ast_["er"]))
                        v_op(lambda e, i=sq_b[kti % 2], m=msl:
                             e.tensor_mul(tdiag, i, m), wv, inc=None)
                        pm = v_op(lambda e, i=sq_b[kti % 2], o=p_b[kti % 2]:
                                  e.tensor_mul(o, tdiag, i))
                    else:
                        pm = v_op(lambda e, i=sq_b[kti % 2], o=p_b[kti % 2]:
                                  e.tensor_mul(o, i, i), wv)
                    p_mark[kti] = pm
                    ast_["kti"] = kti + 1
                    if kt < ktn - 1:
                        return
                    kti = ast_["kti"]
                    emit_pv(ktn - 1, kti - 1)
                    tsm = v_op(lambda e: e.tensor_scalar_mul(rd_b, psD[0:1, :],
                                                             DEN_SC),
                               [("t", dn_mark[kti - 1])])
                    bc1 = t_op(lambda e: e.matmul(psD[:], scvec[:], rd_b,
                                                  start=True, stop=True),
                               [("v", tsm)])
                    ast_["bcm"] = bc1
                    r0sb = sq_b[kti % 2]
                    lr = s_op(lambda e: e.activation(tdiag, psD[:], AF.Ln,
                                                     bias=0.0, scale=1.0),
                              [("t", bc1)])
                    er = s_op(lambda e, o=r0sb: e.activation(o, tdiag, AF.Exp,
                                                             bias=0.0, scale=-1.0),
                              [("s", lr)])
                    ast_["er"] = er
                    v_op(lambda e, i1=r0sb: e.scalar_tensor_tensor(
                        tdiag, psD[:], -1.0, i1, op0=ALU.mult, op1=ALU.mult),
                        [("s", er)], inc=None)
                    rcm = v_op(lambda e, i1=r0sb: e.scalar_tensor_tensor(
                        psD[:], tdiag, 2.0, i1, op0=ALU.add, op1=ALU.mult))
                    ast_["rcm"] = rcm
                    nem = s_op(lambda e: e.copy(num_sb, psN[:]),
                               [("t", pv_mark[kti - 1])])
                    ast_["nem"] = nem
                    ast_["aom"] = v_op(
                        lambda e,
                        o=ao[:, h * S + qc * 512: h * S + (qc + 1) * 512]:
                        e.tensor_tensor(o, num_sb, psD[:], op=ALU.mult),
                        [("s", nem)])

                for kt in range(ktn):
                    ktq.append((unit, kt))

            for it in range(n_iters):
                g = h * n_iters + it
                nt = 2 * it
                w = [("wl", wb_mark[h]), ("v", u_ready_v)]
                if g >= 1:
                    w.append(("v", ropeg[g - 1]))
                    w.append(("s", vevg[g - 1]))
                else:
                    w.append(("t", bcm))
                    w.append(("v", ao_mult_v))
                    w.append(("s", num_evac_s))
                mmm = None
                for sub in range(2):
                    base = 512 * sub
                    ntt = nt + sub
                    for dc in range(DC):
                        last = (sub == 1 and dc == DC - 1)
                        mmm = t_op(lambda e, l=u3[:, dc, ntt * 128:(ntt + 1) * 128],
                                   r=wbf[par][:, dc * 384:(dc + 1) * 384],
                                   o=psA[:, base:base + 384],
                                   st=(dc == 0), sp=(dc == DC - 1):
                                   e.matmul(o, l, r, start=st, stop=sp),
                                   w if (sub == 0 and dc == 0) else (),
                                   inc="t" if last else None)
                        if dc % 4 == 3:
                            drain(1)
                it_marks[(it, "qkv")] = mmm
                if head_stage >= 4 and pend_tr[0] is not None:
                    trfn, pit, qcfn = pend_tr[0]
                    pend_tr[0] = None
                    trfn(pit)
                    if do_attn and pit % 2 == 1:
                        pend_qc.append((qcfn, pit // 2, g))
                if head_stage >= 4 and pend_qc and pend_qc[0][2] < g:
                    qcfn2, j2, _ = pend_qc.pop(0)
                    qcfn2(j2)

                vw = [("t", mmm)]
                if h == 0 and it == 0:
                    vw += extra_entry
                vdst = vx[par][:, nt * 128:(nt + 2) * 128] if it < 2 else \
                    vtm[:, nt * 128:(nt + 2) * 128]
                vev = s_op(lambda e, o=vdst: e.copy(o, v_ps), vw)
                it_marks[(it, "vev")] = vev
                vevg[g] = vev

                qrap = qr
                if head_stage < 1:
                    it_marks[(it, "ropeV")] = C["v"]
                    it_marks[(it, "sqm")] = C["s"]
                    it_marks[(it, "trm")] = C["t"]
                    it_marks[(it, "gbe")] = C["s"]
                    ropeg[g] = C["v"]
                    sqmg[g] = C["s"]
                    trmg[g] = C["t"]
                    gbeg[g] = C["s"]
                    continue
                mw = [("t", mmm), ("wl", tables_wl), ("s", vev)]
                if g >= 1:
                    mw.append(("s", sqmg[g - 1]))
                    mw.append(("t", trmg[g - 1]))
                if h == 0 and it == 0:
                    mw += extra_entry
                mc = None
                for a_ in range(2):
                    for t_ in range(2):
                        qcol = a_ * 256 + t_ * 128
                        pcol = t_ * 512 + a_ * 128
                        scol = (nt + t_) * 128
                        mc = v_op(lambda e, o=tmpc[:, qcol:qcol + 128],
                                  i0=psA[:, pcol:pcol + 128],
                                  i1=ctok[:, scol:scol + 128]:
                                  e.tensor_tensor(o, i0, i1, op=ALU.mult),
                                  mw if (a_ == 0 and t_ == 0) else (), inc=None)
                ms2 = None
                for a_ in range(2):
                    for t_ in range(2):
                        qcol = a_ * 256 + t_ * 128
                        pcol = t_ * 512 + a_ * 128
                        scol = (nt + t_) * 128
                        ms2 = v_op(lambda e, o=tmps[:, qcol:qcol + 64],
                                   i0=psA[:, pcol + 64:pcol + 128],
                                   i1=ssg[:, scol:scol + 64]:
                                   e.tensor_tensor(o, i0, i1, op=ALU.mult),
                                   inc=None)
                        ms2 = v_op(lambda e, o=tmps[:, qcol + 64:qcol + 128],
                                   i0=psA[:, pcol:pcol + 64],
                                   i1=ssg[:, scol + 64:scol + 128]:
                                   e.tensor_tensor(o, i0, i1, op=ALU.mult),
                                   inc="v" if (a_ == 1 and t_ == 1) else None)
                it_marks[(it, "ropeV")] = ms2
                ropeg[g] = ms2
                if head_stage < 2:
                    it_marks[(it, "sqm")] = C["s"]
                    it_marks[(it, "trm")] = C["t"]
                    it_marks[(it, "gbe")] = C["s"]
                    sqmg[g] = C["s"]
                    trmg[g] = C["t"]
                    gbeg[g] = C["s"]
                    continue
                addm = None
                for j in range(4):
                    addm = v_op(lambda e, o=qrap[:, j * 128:(j + 1) * 128],
                                i0=tmps[:, j * 128:(j + 1) * 128],
                                i1=tmpc[:, j * 128:(j + 1) * 128]:
                                e.scalar_tensor_tensor(o, i0, 1.0, i1, op0=ALU.mult,
                                                       op1=ALU.add),
                                inc="v" if j == 3 else None)
                sqm = s_op(lambda e, i=qrap: e.activation(tmpc, i, AF.Square,
                                                          bias=0.0, scale=1.0),
                           [("v", addm)])
                qr3 = qrap.rearrange("p (g f) -> p g f", g=4)
                tc3 = tmpc.rearrange("p (g f) -> p g f", g=4)
                v_op(lambda e, i=qr3: e.reduce_sum(q4, i, axis=AXX), inc=None)
                v_op(lambda e, i=tc3: e.reduce_sum(qs4, i, axis=AXX),
                     [("s", sqm)], inc=None)
                vsync()
                if os.environ.get("KCOLL") and h == 0 and it == int(os.environ.get("KIT", "0")):
                    v_op(lambda e: e.tensor_copy(stat[:, 368:372], q4), inc=None)
                    v_op(lambda e: e.tensor_copy(stat[:, 372:376], qs4), inc=None)
                    s_op(lambda e, i=qrap: e.copy(tdiag, i), [("v", C["v"])], inc=None)
                v_op(lambda e: e.tensor_scalar_mul(m4, q4, 1.0 / HD), inc=None)
                v_op(lambda e: e.tensor_scalar_mul(v4, qs4, 1.0 / HD), inc=None)
                vsync()
                v_op(lambda e: e.tensor_mul(m4sq, m4, m4), inc=None)
                vsync()
                vv4 = v_op(lambda e: e.tensor_sub(v4, v4, m4sq))
                l4 = s_op(lambda e: e.activation(mr4, v4, AF.Ln, bias=epsc[:, 0:1],
                                                 scale=1.0), [("v", vv4)])
                em4 = s_op(lambda e: e.activation(rstd4, mr4, AF.Exp, bias=0.0,
                                                  scale=nhc[:, 0:1]), [("s", l4)])
                mrm = v_op(lambda e: e.tensor_mul(mr4, m4, rstd4), [("s", em4)])
                if os.environ.get("KCOLL") and h == 0 and it == int(os.environ.get("KIT", "0")):
                    v_op(lambda e: e.tensor_copy(stat[:, 376:380], v4), inc=None)
                    mrm = v_op(lambda e: e.tensor_copy(stat[:, 380:384], rstd4), [("v", mrm)])
                if head_stage < 3:
                    it_marks[(it, "sqm")] = sqm
                    it_marks[(it, "trm")] = C["t"]
                    it_marks[(it, "gbe")] = C["s"]
                    sqmg[g] = sqm
                    trmg[g] = C["t"]
                    gbeg[g] = C["s"]
                    continue
                K.emit("vector", lambda e: e.engine_nop(), [("v", mrm)], "vs", 1)
                apm = None
                for j in range(4):
                    apm = v_op(lambda e, o=tmps[:, j * 128:(j + 1) * 128],
                               i0=qrap[:, j * 128:(j + 1) * 128],
                               r=rstd4[:, j:j + 1], m=mr4[:, j:j + 1]:
                               e.tensor_scalar(o, i0, r, m, op0=ALU.mult,
                                               op1=ALU.subtract),
                               [("vs", C["vs"])] if j == 0 else (),
                               inc="v" if j == 3 else None)
                it_marks[(it, "sqm")] = sqm
                sqmg[g] = sqm
                it_marks[(it, "apm")] = apm
                if head_stage < 4:
                    it_marks[(it, "trm")] = C["t"]
                    it_marks[(it, "gbe")] = C["s"]
                    trmg[g] = C["t"]
                    gbeg[g] = C["s"]
                    continue
                pend_tr[0] = (emit_tr, it, sched_qc)
            prev_qkv_w_reader[h] = it_marks[(n_iters - 1, "qkv")]
            if not do_attn:
                if head_stage >= 4 and pend_tr[0] is not None:
                    trfn, pit, qcfn = pend_tr[0]
                    pend_tr[0] = None
                    trfn(pit)
                num_evac_s = it_marks[(n_iters - 1, "gbe")]
                ao_mult_v = C["v"]
                bcm = C["t"]
                last_ao_v = C["v"]
                continue

        if head_stage >= 4 and pend_tr[0] is not None:
            trfn, pit, qcfn = pend_tr[0]
            pend_tr[0] = None
            trfn(pit)
            if do_attn and pit % 2 == 1:
                pend_qc.append((qcfn, pit // 2, 0))
        while pend_qc:
            qcfn2, j2, _ = pend_qc.pop(0)
            qcfn2(j2)
        drain(len(ktq))

        last_qkv_mm[0] = prev_qkv_w_reader[n_heads - 1]
        if do_attn and head_stage >= 4 and n_heads:
            bcm = ast_["bcm"]
            ao_mult_v = ast_["aom"]
            num_evac_s = ast_["nem"]
            last_ao_v = ao_mult_v
        attn_last["t"] = bcm
        attn_last["v"] = ao_mult_v
        attn_last["s"] = num_evac_s
        return last_ao_v

    def sched_ph5(seq, ao_ready_v):
        pw = [("t", last_qkv_mm[0]), ("so", ph0_marks[len(chunks) - 1]),
              ("v", ao_ready_v)]
        K.dma("sync", uh, wo_scr.rearrange("h p f -> p h f"), pw, inc="wl")
        K.dma("sync", ffnA[:], fa_scr, inc="wl")
        K.dma("sync", wrA[:], wr_scr, inc="wl")
        K.dma("sync", ffnB[:], fb_scr, inc="wl")
        pro_wl = C["wl"]
        atl = attn_last["t"]

        r1_mark = {}
        bev_mark = {}
        zmul_mark = {}
        lg_mark = {}
        store_mark = {}
        wfm_evac = {}
        app_last = {}
        w_fm3 = w_fm.rearrange("p (dc f) -> p dc f", dc=DC)
        oc_bank = [psA[:, 0:512], psA[:, 512:1024], psS[:, 0:512], psS[:, 512:1024]]
        avst = {}
        avq = []

        def adrain(n):
            while n > 0 and avq:
                fn, hh = avq.pop(0)
                fn(hh)
                n -= 1

        def enq_av(nt):
            def chunk(hh, nt=nt):
                w = []
                if hh == 0:
                    w = [("wl", pro_wl), ("v", ao_ready_v)]
                    if nt >= 1:
                        for oc in range(OC):
                            w.append(("v", r1_mark[(nt - 1, oc)]))
                for oc in range(OC):
                    m = t_op(lambda e,
                             l=ao[:, hh * S + nt * 128: hh * S + (nt + 1) * 128],
                             r=uh[:, hh, oc * 512:(oc + 1) * 512],
                             o=oc_bank[oc], st=(hh == 0), sp=(hh == H - 1):
                             e.matmul(o, l, r, start=st, stop=sp),
                             w if oc == 0 else (),
                             inc="t" if hh == H - 1 else None)
                    if hh == H - 1:
                        avst[oc] = m
            for hh in range(H):
                avq.append((chunk, hh))

        xbuf = [xst, xstB]
        xa_mark = {}

        def issue_x(ntt, w=()):
            b = xbuf[ntt % 2]
            m0 = K.dma("gpsimd", b[0],
                       x_in[seq * S + ntt * 128: seq * S + (ntt + 1) * 128, 0:1024],
                       list(w), inc="xa")
            m1 = K.dma("gpsimd", b[1],
                       x_in[seq * S + ntt * 128: seq * S + (ntt + 1) * 128,
                            1024:2048], inc="xa")
            xa_mark[ntt] = (m0, m1)

        issue_x(0, [("t", atl), ("v", ao_ready_v)])
        issue_x(1, ())
        enq_av(0)
        for nt in range(NT):
            par = nt % 2
            xb = xbuf[nt % 2]
            xm0, xm1 = xa_mark[nt]
            adrain(len(avq))
            for oc in range(OC):
                xh = xb[oc // 2][:, (oc % 2) * 512:(oc % 2) * 512 + 512]
                rw = [("t", avst[oc]), ("xa", xm1 if oc >= 2 else xm0),
                      ("v", ao_ready_v)]
                if (nt - 1) in app_last and oc == 0:
                    rw.append(("s", app_last[nt - 1]))
                rm = v_op(lambda e, o=r1[:, oc * 512:(oc + 1) * 512], i0=oc_bank[oc],
                          i1=xh:
                          e.scalar_tensor_tensor(o, i0, 1.0, i1, op0=ALU.mult,
                                                 op1=ALU.add), rw)
                r1_mark[(nt, oc)] = rm
            if nt + 2 < NT:
                issue_x(nt + 2, [("v", r1_mark[(nt, 3)])])
            if nt + 1 < NT:
                enq_av(nt + 1)
            adrain(7)
            rm3 = r1_mark[(nt, 3)]
            xsum_s = SLOT["xsum" + str(par)]
            var_s = SLOT["var" + str(par)]
            rstd_s = SLOT["rstd" + str(par)]
            mr_s = SLOT["mr" + str(par)]
            vrm = v_op(lambda e, o=xsum_s: e.reduce_sum(o, r1, axis=AXX),
                       [("v", rm3)])
            sw = [("v", rm3), ("t", atl)]
            ssm = s_op(lambda e, a=var_s: e.activation(w_tm, r1, AF.Square, bias=0.0,
                                                       scale=1.0, accum_out=a[:, 0:1]),
                       sw)
            v_op(lambda e, i=xsum_s: e.tensor_scalar_mul(SLOT["tmp0"], i, 1.0 / D),
                 [("s", ssm), ("v", vrm)], inc=None)
            v_op(lambda e, i=var_s: e.tensor_scalar_mul(SLOT["tmp1"], i, 1.0 / D),
                 inc=None)
            vsync()
            v_op(lambda e: e.tensor_mul(SLOT["tmp2"], SLOT["tmp0"], SLOT["tmp0"]),
                 inc=None)
            vsync()
            vvm = v_op(lambda e, o=var_s: e.tensor_sub(o, SLOT["tmp1"], SLOT["tmp2"]))
            l2m = s_op(lambda e, i=var_s: e.activation(SLOT["lntmp"], i, AF.Ln,
                                                       bias=epsc[:, 0:1], scale=1.0),
                       [("v", vvm)])
            e2m = s_op(lambda e, o=rstd_s: e.activation(o, SLOT["lntmp"], AF.Exp,
                                                        bias=0.0, scale=nhc[:, 0:1]),
                       [("s", l2m)])
            nmm = v_op(lambda e, o=mr_s, r=rstd_s: e.tensor_scalar(
                o, SLOT["tmp0"], r[:, 0:1], -1.0, op0=ALU.mult, op1=ALU.mult),
                       [("s", e2m)])
            apm_g = {}
            for g in range(4):
                apm_g[g] = s_op(lambda e, b=mr_s, r=rstd_s,
                                o=w_tm[:, g * 512:(g + 1) * 512],
                                i=r1[:, g * 512:(g + 1) * 512]:
                                e.activation(o, i, AF.Identity, bias=b[:, 0:1],
                                             scale=r[:, 0:1]),
                                [("v", nmm)] if g == 0 else ())
            app_last[nt] = apm_g[3]
            for g in range(4):
                half = g % 2
                gw = [("s", apm_g[g])]
                prev = (nt, g - 2) if g >= 2 else (nt - 1, g + 2)
                if prev in wfm_evac:
                    gw.append(("s", wfm_evac[prev]))
                tm = None
                for j in range(4):
                    dc = 4 * g + j
                    tm = t_op(lambda e, i=w_tm[:, dc * 128:(dc + 1) * 128],
                              o=psTb[:, half * 1024 + j * 128: half * 1024 + (j + 1) * 128]:
                              e.transpose(o, i, ident[:]), gw if j == 0 else (),
                              inc="t" if j == 3 else None)
                adrain(1)
                em = s_op(lambda e, o=w_fm3[:, 4 * g:4 * (g + 1), :],
                          i=psTb[:, half * 1024: half * 1024 + 512].rearrange(
                              "p (a b) -> p a b", a=4):
                          e.copy(o, i), [("t", tm)])
                wfm_evac[(nt, g)] = em
            hw = [("s", wfm_evac[(nt, 0)])]
            if nt >= 1:
                hw.append(("v", zmul_mark[nt - 1]))
                hw.append(("v", lg_mark[nt - 1]))
                hw.append(("v", bev_mark[(nt - 1, 3)]))
            for dc in range(DC):
                hwd = hw if dc == 0 else (
                    [("s", wfm_evac[(nt, dc // 4)])] if dc % 4 == 0 else ())
                t_op(lambda e, l=w_fm[:, dc * 128:(dc + 1) * 128],
                     r=ffnA[:, dc * 512:(dc + 1) * 512], o=psN[:],
                     st=(dc == 0), sp=(dc == DC - 1):
                     e.matmul(o, l, r, start=st, stop=sp), hwd,
                     inc=None)
                t_op(lambda e, l=w_fm[:, dc * 128:(dc + 1) * 128],
                     r=wrA[:, dc * 8:(dc + 1) * 8], o=psD[:, 0:8],
                     st=(dc == 0), sp=(dc == DC - 1):
                     e.matmul(o, l, r, start=st, stop=sp),
                     inc="t" if dc == DC - 1 else None)
                if dc % 2 == 1:
                    adrain(1)
            lmm = C["t"]
            lgm = v_op(lambda e: e.tensor_tensor(logit_sb, psD[:, 0:8], brx[:],
                                                 op=ALU.add),
                       [("t", lmm), ("wl", wl_consts)])
            lg_mark[nt] = lgm
            vsync()
            ngm = v_op(lambda e: e.reduce_max(SLOT["negm"], logit_sb, axis=AXX,
                                              negate=True))
            exm = s_op(lambda e: e.activation(esb8, logit_sb, AF.Exp,
                                              bias=SLOT["negm"][:, 0:1], scale=1.0),
                       [("v", ngm)])
            v_op(lambda e: e.reduce_sum(SLOT["ssum"], esb8, axis=AXX),
                 [("s", exm)], inc=None)
            vsync()
            v_op(lambda e: e.tensor_scalar_mul(SLOT["w64"], SLOT["ssum"], float(RK)),
                 inc=None)
            vsync()
            v_op(lambda e: e.reciprocal(SLOT["w64"], SLOT["w64"]), inc=None)
            v_op(lambda e: e.tensor_scalar(mask8, logit_sb, SLOT["negm"][:, 0:1], 0.0,
                                           op0=ALU.add, op1=ALU.is_equal), inc=None)
            vsync()
            v_op(lambda e: e.tensor_copy(mask_exp.rearrange("p (a b) -> p a b", a=8),
                                         mask8.unsqueeze(2).broadcast_to([128, 8, 64])),
                 inc=None)
            zm = v_op(lambda e: e.tensor_tensor(z_tm, psN[:], mask_exp, op=ALU.mult))
            zmul_mark[nt] = zm
            adrain(2)
            ztm = None
            for j in range(ERC):
                zw = [("v", zm)]
                if zfm_s[0]:
                    zw.append(("s", zfm_s[0]))
                ztm = t_op(lambda e, i=z_tm[:, j * 128:(j + 1) * 128],
                           o=psTb[:, j * 128:(j + 1) * 128]:
                           e.transpose(o, i, ident[:]), zw if j == 0 else (),
                           inc="t" if j == ERC - 1 else None)
            zem = s_op(lambda e: e.copy(z_fm, psTb[:, 0:512]), [("t", ztm)])
            zfm_s[0] = zem
            bst = {}
            for oc in range(OC):
                bw = [("s", zem)] if oc == 0 else [("v", bev_mark[(nt, oc - 1)])]
                for erc in range(ERC):
                    m = t_op(lambda e, l=z_fm[:, erc * 128:(erc + 1) * 128],
                             r=ffnB[:, erc * 2048 + oc * 512:
                                     erc * 2048 + (oc + 1) * 512],
                             o=psN[:], st=(erc == 0), sp=(erc == ERC - 1):
                             e.matmul(o, l, r, start=st, stop=sp),
                             bw if erc == 0 else (),
                             inc="t" if erc == ERC - 1 else None)
                    if erc == ERC - 1:
                        bst[oc] = m
                adrain(1)
                idx = nt * 4 + oc
                ow = [("t", bst[oc])]
                if idx >= 2:
                    ow.append(("so", store_mark[idx - 2]))
                bm = v_op(lambda e, o=out_sb[oc % 2], i0=psN[:],
                          i1=r1[:, oc * 512:(oc + 1) * 512]:
                          e.scalar_tensor_tensor(o, i0, SLOT["w64"][:, 0:1], i1,
                                                 op0=ALU.mult, op1=ALU.add), ow)
                bev_mark[(nt, oc)] = bm
                store_mark[idx] = K.dma(
                    "gpsimd",
                    y_out[seq * S + nt * 128: seq * S + (nt + 1) * 128,
                          oc * 512:(oc + 1) * 512],
                    out_sb[oc % 2], [("v", bm)], inc="so")
            ph5_last["t"] = bst[OC - 1]
        return store_mark[NT * 4 - 1]

    dump = os.environ.get("KDUMP", "")

    def sched_dump(src_buf, gate):
        yb = y_out.bitcast(BF16)
        for c in range(8):
            K.dma("gpsimd", yb[c * 128:(c + 1) * 128, :],
                  src_buf[:, c * 4096:(c + 1) * 4096], [gate], inc="so")

    for seq in range(n_seqs):
        if seq == 0:
            u_ready = sched_ph1(0, interleave_ph0=True)
            ph0_all = [("so", ph0_marks[len(chunks) - 1])]
            if n_heads:
                ao_ready = sched_heads(0, u_ready, extra_entry=ph0_all)
        else:
            u_ready = sched_ph1(seq, interleave_ph0=False, entry_t=C["t"])
            if n_heads:
                ao_ready = sched_heads(seq, u_ready)
        if seq == 0 and dump == "u":
            sched_dump(u_wout[:], ("v", u_ready))
            yb2 = y_out.bitcast(BF16)
            K.dma("gpsimd", y_out[1500:1628, 0:32], stat[:, 0:32], [("v", u_ready)], inc="so")
            K.dma("gpsimd", yb2[1700:1828, 0:2048], u_tm[0][:], [("v", u_ready)], inc="so")
            K.dma("gpsimd", yb2[1900:2028, 0:2048], u_tm[1][:], [("v", u_ready)], inc="so")
            K.dma("gpsimd", y_out[2100:2228, :], x_t[0][:], [("v", u_ready)], inc="so")
            K.dma("gpsimd", y_out[2300:2428, :], x_t[1][:], [("v", u_ready)], inc="so")
            break
        if seq == 0 and dump == "ao" and n_heads:
            sched_dump(ao[:], ("v", ao_ready))
            break
        if seq == 0 and dump == "qk" and n_heads:
            g8 = ("s", C["s"])
            gv = ("v", C["v"])
            yb3 = y_out.bitcast(BF16)
            K.dma("gpsimd", yb3[0:128, 0:2048], qfm[0][:], [g8], inc="so")
            K.dma("gpsimd", yb3[200:328, 0:2048], kfm[0][:], [g8], inc="so")
            K.dma("gpsimd", yb3[400:528, 0:2048], vtm[:], [g8], inc="so")
            K.dma("gpsimd", yb3[600:728, 0:512], qr[:], [gv], inc="so")
            K.dma("gpsimd", yb3[800:928, 0:512], vx[0][:], [gv], inc="so")
            K.dma("gpsimd", yb3[1000:1128, 0:512], tmpc[:], [gv], inc="so")
            K.dma("gpsimd", y_out[1200:1328, 0:128], stat[:, 256:384], [gv], inc="so")
            K.dma("gpsimd", y_out[1400:1528, 0:512], tdiag, [gv], inc="so")
            break
        if n_heads and do_ph5:
            sched_ph5(seq, ao_ready)

    total_so = C["so"]
    K.emit("gpsimd", lambda e: e.engine_nop(), [("so", total_so)], None)

    with es:
        K.build()
    return nc


def _host_layouts(inputs):
    wqkv = np.ascontiguousarray(inputs["L0_attn_w_qkv"], dtype=np.float32)
    wout = np.ascontiguousarray(inputs["L0_attn_w_out"], dtype=np.float32)
    lns = np.ascontiguousarray(inputs["L0_attn_ln_scale"], dtype=np.float32)
    lnb = np.ascontiguousarray(inputs["L0_attn_ln_bias"], dtype=np.float32)
    A = np.ascontiguousarray(inputs["L0_ffn_A"], dtype=np.float32)
    Bw = np.ascontiguousarray(inputs["L0_ffn_B"], dtype=np.float32)
    Wr = np.ascontiguousarray(inputs["L0_ffn_Wr"], dtype=np.float32)
    br = np.ascontiguousarray(inputs["L0_ffn_br"], dtype=np.float32)

    d = {}
    d["wqkv"] = np.ascontiguousarray(
        wqkv.reshape(DC, 128, 3, H, 128).transpose(3, 0, 1, 2, 4).reshape(H, DC, 128, 384))
    d["wout"] = np.ascontiguousarray(wout.reshape(H, 128, D))
    d["ffnA"] = np.ascontiguousarray(
        A.transpose(2, 0, 1).reshape(D, ER).reshape(DC, 128, ER).transpose(1, 0, 2)
        .reshape(128, DC * ER))
    d["wr"] = np.ascontiguousarray(
        Wr.T.reshape(DC, 128, NE).transpose(1, 0, 2).reshape(128, DC * NE))
    d["ffnB"] = np.ascontiguousarray(
        Bw.transpose(0, 2, 1).reshape(ER, D).reshape(ERC, 128, D).transpose(1, 0, 2)
        .reshape(128, ERC * D))
    d["ident"] = np.eye(128, dtype=ml_dtypes.bfloat16)
    inv = 1.0 / (10000.0 ** (np.arange(0, HD, 2, dtype=np.float64) / HD))
    t = np.arange(S, dtype=np.float64)[:, None] * inv[None, :]
    cosv = np.cos(t).astype(np.float32)
    sinv = np.sin(t).astype(np.float32)
    Cfull = np.concatenate([cosv, cosv], axis=1)
    Ssg = np.concatenate([-sinv, sinv], axis=1)
    d["ctok"] = np.ascontiguousarray(
        Cfull.reshape(NT, 128, HD).transpose(1, 0, 2).reshape(128, NT * HD))
    d["ssg"] = np.ascontiguousarray(
        Ssg.reshape(NT, 128, HD).transpose(1, 0, 2).reshape(128, NT * HD))
    mw = (np.arange(896, dtype=np.int64)[None, :] >=
          (np.arange(128, dtype=np.int64)[:, None] + 384)).astype(ml_dtypes.bfloat16)
    d["maskw"] = np.ascontiguousarray(mw)
    d["gcolv"] = np.ascontiguousarray(lns.reshape(128, 1))
    d["bcolv"] = np.ascontiguousarray(lnb.reshape(128, 1))
    d["brx"] = np.ascontiguousarray(np.broadcast_to(br[None, :], (128, 8)))
    d["onesb"] = np.ones((128, 1), dtype=ml_dtypes.bfloat16)
    d["epsc"] = np.full((128, 1), EPS, dtype=np.float32)
    d["nhc"] = np.full((128, 1), -0.5, dtype=np.float32)
    d["scv"] = np.full((1, 128), 1.0, dtype=np.float32)
    return d


def kernel(**inputs):
    x = np.ascontiguousarray(inputs["L0_attn_x"], dtype=np.float32)
    shared = _host_layouts(inputs)
    nc = build_program()
    in_maps = []
    for c in range(NC):
        m = dict(shared)
        m["x"] = np.ascontiguousarray(x[2 * c: 2 * c + 2].reshape(SEQS * S, D))
        in_maps.append(m)
    res = run_bass_kernel_spmd(nc, in_maps, core_ids=list(range(NC)))
    out = np.concatenate([r["y"].reshape(SEQS, S, D) for r in res.results], axis=0)
    return out


# version 20
# speedup vs baseline: 1.0177x; 1.0177x over previous
import os
import numpy as np
import ml_dtypes
import concourse.bass as bass
import concourse.mybir as mybir
from concourse.bass_utils import run_bass_kernel_spmd

F32 = mybir.dt.float32
BF16 = mybir.dt.bfloat16
U8 = mybir.dt.uint8
AF = mybir.ActivationFunctionType
ALU = mybir.AluOpType
AXX = mybir.AxisListType.X

NC = 8
SEQS = 2
S = 2048
D = 2048
H = 16
HD = 128
NT = 16
DC = 16
EPS = 1e-5
NE, RK = 8, 64
ER = NE * RK
ERC = 4
OC = 4
DEN_SC = 1.0

ENGINES = ("sync", "scalar", "vector", "tensor", "gpsimd")


class KB:
    def __init__(self, nc):
        self.nc = nc
        self.ops = {e: [] for e in ENGINES}
        self.sems = {}
        self.cnt = {}
        self._ctx = []

    def sem(self, name):
        if name not in self.sems:
            cm = self.nc.semaphore(name)
            self.sems[name] = cm.__enter__()
            self._ctx.append(cm)
            self.cnt[name] = 0
        return name

    def emit(self, eng, fn, waits=(), inc=None, inc_n=1):
        ws = tuple((s, v) for (s, v) in waits if v > 0)
        self.ops[eng].append((fn, ws, inc, inc_n))
        if inc is not None:
            self.cnt[inc] += inc_n
            return self.cnt[inc]
        return None

    def dma(self, eng, out, in_, waits=(), inc=None):
        def fn(e, out=out, in_=in_):
            return e.dma_start(out=out, in_=in_)
        return self.emit(eng, fn, waits, inc, 16)

    def build(self):
        with self.nc.Block() as blk:
            for eng in ENGINES:
                getattr(blk, eng)(self._mk(eng))

    def _mk(self, eng):
        ops = self.ops[eng]
        sems = self.sems

        def body(e):
            seen = {}
            for fn, waits, inc, inc_n in ops:
                for s_, v_ in waits:
                    if seen.get(s_, 0) < v_:
                        e.wait_ge(sems[s_], v_)
                        seen[s_] = v_
                inst = fn(e)
                if inc is not None:
                    inst.then_inc(sems[inc], inc_n)
        return body


def build_program(n_heads=H, do_ph5=True, n_seqs=SEQS, do_attn=True, head_stage=4, n_iters=8):
    nc = bass.Bass("TRN2", target_bir_lowering=False, detect_race_conditions=False)

    x_in = nc.dram_tensor("x", [SEQS * S, D], F32, kind="ExternalInput").ap()
    y_out = nc.dram_tensor("y", [SEQS * S, D], F32, kind="ExternalOutput").ap()
    wqkv_in = nc.dram_tensor("wqkv", [H, DC, 128, 384], F32, kind="ExternalInput").ap()
    wout_in = nc.dram_tensor("wout", [H, 128, D], F32, kind="ExternalInput").ap()
    ffnA_in = nc.dram_tensor("ffnA", [128, DC * ER], F32, kind="ExternalInput").ap()
    wr_in = nc.dram_tensor("wr", [128, DC * NE], F32, kind="ExternalInput").ap()
    ffnB_in = nc.dram_tensor("ffnB", [128, ERC * D], F32, kind="ExternalInput").ap()
    ident_in = nc.dram_tensor("ident", [128, 128], BF16, kind="ExternalInput").ap()
    ctok_in = nc.dram_tensor("ctok", [128, NT * HD], F32, kind="ExternalInput").ap()
    ssg_in = nc.dram_tensor("ssg", [128, NT * HD], F32, kind="ExternalInput").ap()
    maskw_in = nc.dram_tensor("maskw", [128, 896], BF16, kind="ExternalInput").ap()
    gcol_in = nc.dram_tensor("gcolv", [128, 1], F32, kind="ExternalInput").ap()
    bcol_in = nc.dram_tensor("bcolv", [128, 1], F32, kind="ExternalInput").ap()
    br_in = nc.dram_tensor("brx", [128, 8], F32, kind="ExternalInput").ap()
    onesb_in = nc.dram_tensor("onesb", [128, 1], BF16, kind="ExternalInput").ap()
    eps_in = nc.dram_tensor("epsc", [128, 1], F32, kind="ExternalInput").ap()
    nh_in = nc.dram_tensor("nhc", [128, 1], F32, kind="ExternalInput").ap()
    scv_in = nc.dram_tensor("scv", [1, 128], F32, kind="ExternalInput").ap()

    wq_scr = nc.dram_tensor("wq_scr", [H, 128, DC * 384], BF16, kind="Internal").ap()
    wo_scr = nc.dram_tensor("wo_scr", [H, 128, D], BF16, kind="Internal").ap()
    fa_scr = nc.dram_tensor("fa_scr", [128, DC * ER], BF16, kind="Internal").ap()
    wr_scr = nc.dram_tensor("wr_scr", [128, DC * NE], BF16, kind="Internal").ap()
    fb_scr = nc.dram_tensor("fb_scr", [128, ERC * D], BF16, kind="Internal").ap()

    import contextlib
    es = contextlib.ExitStack()

    def sbuf(name, shape, dt):
        return es.enter_context(nc.sbuf_tensor(name, shape, dt))

    u_wout = sbuf("u_wout", [128, DC * S], BF16)
    ao = sbuf("ao", [128, H * S], BF16)
    ident = sbuf("identb", [128, 128], BF16)
    gcol = sbuf("gcol", [128, 1], F32)
    bcol = sbuf("bcol", [128, 1], F32)
    onesb = sbuf("onesbf", [128, 1], BF16)
    scvec = sbuf("scvec", [1, 128], F32)
    maskw = sbuf("maskwb", [128, 896], BF16)
    brx = sbuf("brxb", [128, 8], F32)
    epsc = sbuf("epscb", [128, 1], F32)
    nhc = sbuf("nhcb", [128, 1], F32)
    stat = sbuf("stat", [128, 384], F32)
    arena = sbuf("arena", [128, 76800], U8)

    psA = es.enter_context(nc.psum_tensor("psA", [128, 1024], F32))
    psT = es.enter_context(nc.psum_tensor("psT", [128, 1024], F32))
    psS = es.enter_context(nc.psum_tensor("psS", [128, 1024], F32))
    psN = es.enter_context(nc.psum_tensor("psN", [128, 512], F32))
    psD = es.enter_context(nc.psum_tensor("psD", [128, 512], F32))
    psTb = psT[:].bitcast(BF16)

    def av(off, nbytes, dt):
        return arena[:, off:off + nbytes].bitcast(dt)

    x_t = [av(0, 8192, F32), av(8192, 8192, F32)]
    sqscr = av(16384, 8192, F32)
    u_tm = [av(33792, 4096, BF16), av(37888, 4096, BF16)]
    wstage = [av(43008, 6144, F32), av(49152, 6144, F32)]
    wcvt = [av(55296, 3072, BF16), av(58368, 3072, BF16)]

    wbf = [av(0, 12288, BF16), av(12288, 12288, BF16)]
    ctok = av(24576, 8192, F32)
    ssg = av(32768, 8192, F32)
    qfm = [av(40960, 4096, BF16), av(68608, 4096, BF16)]
    kfm = [av(45056, 4096, BF16), av(72704, 4096, BF16)]
    vtm = av(49152, 4096, BF16)
    vx = [av(54272, 1024, BF16), av(56320, 1024, BF16)]
    qr = av(53248, 1024, BF16)
    tmpc = av(55296, 1024, BF16)
    p_b = [av(57344, 1024, BF16), av(58368, 1024, BF16)]
    tmps = av(63488, 1024, BF16)
    sq_b = [av(64512, 2048, F32), av(66560, 2048, F32)]
    tdiag = av(59392, 2048, F32)
    rd_full = av(59392, 2048, F32)
    rd_b = rd_full[0:1, :]
    num_sb = av(61440, 1024, BF16)

    ffnA = av(0, 16384, BF16)
    wrA = av(16384, 256, BF16)
    ffnB = av(16640, 16384, BF16)
    r1 = av(33024, 8192, F32)
    xst = [av(41216, 4096, F32), av(45312, 4096, F32)]
    w_tm = av(49408, 4096, BF16)
    w_fm = av(53504, 4096, BF16)
    z_tm = av(57600, 1024, BF16)
    z_fm = av(58624, 1024, BF16)
    logit_sb = av(59648, 32, F32)
    mask8 = av(59680, 16, BF16)
    mask_exp = av(59712, 1024, BF16)
    out_sb = [av(68608, 2048, F32), av(74752, 2048, F32)]
    xstB = [av(60736, 4096, F32), av(70656, 4096, F32)]

    SLOT = {}
    for i, nm in enumerate(["xsum0", "xsum1", "var0", "var1", "rstd0", "rstd1",
                            "mr0", "mr1", "lntmp", "negm", "ssum", "w64",
                            "tmp0", "tmp1", "tmp2", "tmp3"]):
        SLOT[nm] = stat[:, i * 16:i * 16 + 1]
    q4 = stat[:, 256:260]
    qs4 = stat[:, 272:276]
    m4 = stat[:, 288:292]
    v4 = stat[:, 304:308]
    rstd4 = stat[:, 320:324]
    mr4 = stat[:, 336:340]
    esb8 = stat[:, 352:360]
    m4sq = stat[:, 360:364]

    K = KB(nc)
    for s_ in ["t", "v", "s", "xa", "wl", "so", "sq", "vs"]:
        K.sem(s_)
    C = K.cnt

    def t_op(fn, waits=(), inc="t"):
        return K.emit("tensor", fn, waits, inc)

    def v_op(fn, waits=(), inc="v"):
        return K.emit("vector", fn, waits, inc)

    def s_op(fn, waits=(), inc="s"):
        return K.emit("scalar", fn, waits, inc)

    def vsync():
        m = K.emit("vector", lambda e: e.engine_nop(), (), "vs", 1)
        K.emit("vector", lambda e: e.engine_nop(), [("vs", m)], None)

    u3 = u_wout[:].rearrange("p (dc s) -> p dc s", dc=DC)
    uh = u_wout[:].rearrange("p (h f) -> p h f", h=H)
    ct3 = ctok.rearrange("p (nt f) -> p nt f", nt=NT)
    sg3 = ssg.rearrange("p (nt f) -> p nt f", nt=NT)
    psA4 = psA[:].rearrange("p (t g f) -> p g t f", t=2, g=8)
    psA4 = psA[:].rearrange("p (t g f) -> p g t f", t=2, g=4)
    qk_ps = psA4[:, 0:2]
    v_ps = psA4[:, 2]

    chunks = []
    for h in range(H):
        for g in range(4):
            chunks.append((wqkv_in[h, 4 * g:4 * (g + 1)].rearrange("c p f -> p c f"),
                           wq_scr[h][:, g * 1536:(g + 1) * 1536], 1536))
    for h in range(H):
        for g in range(2):
            chunks.append((wout_in[h][:, g * 1024:(g + 1) * 1024],
                           wo_scr[h][:, g * 1024:(g + 1) * 1024], 1024))
    for g in range(8):
        chunks.append((ffnA_in[:, g * 1024:(g + 1) * 1024],
                       fa_scr[:, g * 1024:(g + 1) * 1024], 1024))
    chunks.append((wr_in[:, 0:128], wr_scr[:, 0:128], 128))
    for g in range(8):
        chunks.append((ffnB_in[:, g * 1024:(g + 1) * 1024],
                       fb_scr[:, g * 1024:(g + 1) * 1024], 1024))

    K.dma("sync", ident[:], ident_in, inc="wl")
    K.dma("sync", gcol[:], gcol_in, inc="wl")
    K.dma("sync", bcol[:], bcol_in, inc="wl")
    K.dma("sync", onesb[:], onesb_in, inc="wl")
    K.dma("sync", scvec[:], scv_in, inc="wl")
    K.dma("sync", maskw[:], maskw_in, inc="wl")
    K.dma("sync", brx[:], br_in, inc="wl")
    K.dma("sync", epsc[:], eps_in, inc="wl")
    K.dma("sync", nhc[:], nh_in, inc="wl")
    wl_consts = C["wl"]

    ph0_marks = {}
    ph0_cv = {}

    def sched_ph0_chunk(i):
        src, dst, ncols = chunks[i]
        par = i % 2
        stg = wstage[par][:, 0:ncols]
        cvt = wcvt[par][:, 0:ncols]
        waits = []
        if i >= 2:
            waits.append(ph0_cv[i - 2])
        ldm = K.dma("sync", stg, src, waits, inc="wl")
        cwaits = [("wl", ldm)]
        if i >= 2:
            cwaits.append(("so", ph0_marks[i - 2]))
        if i % 2 == 0:
            cm = v_op(lambda e, o=cvt, s2=stg: e.tensor_copy(o, s2), cwaits)
            ph0_cv[i] = ("v", cm)
        else:
            cm = s_op(lambda e, o=cvt, s2=stg: e.copy(o, s2), cwaits)
            ph0_cv[i] = ("s", cm)
        ph0_marks[i] = K.dma("gpsimd", dst, cvt, [ph0_cv[i]], inc="so")

    def ph0_head_ready(h):
        return ph0_marks[4 * h + 3]

    last_qkv_mm = [0]
    attn_last = {"t": 0, "v": 0, "s": 0}
    ph5_last = {"t": 0, "v": 0, "s": 0, "so": 0}
    zfm_s = [0]

    def sched_ph1(seq, interleave_ph0, entry_t=0):
        ph0_i = [0]

        def maybe_ph0(k=2):
            for _ in range(k):
                if interleave_ph0 and ph0_i[0] < len(chunks):
                    sched_ph0_chunk(ph0_i[0])
                    ph0_i[0] += 1

        apply_mark = {}
        trans_mark = {}
        evac_mark = {}
        red_mark = {}
        for nt in range(NT):
            par = nt % 2
            maybe_ph0()
            waits = [("t", entry_t)] if nt < 2 else \
                [("s", apply_mark[nt - 2]), ("v", red_mark[nt - 2])]
            xm = K.dma("gpsimd", x_t[par],
                       x_in[seq * S + nt * 128: seq * S + (nt + 1) * 128, :], waits, inc="xa")
            rm = v_op(lambda e, x=x_t[par], o=SLOT["xsum" + str(par)]:
                      e.reduce_sum(o, x, axis=AXX), [("xa", xm)])
            red_mark[nt] = rm
            sw = [("xa", xm)]
            if nt < 2 and entry_t:
                sw.append(("t", entry_t))
            sm = s_op(lambda e, x=x_t[par], a=SLOT["var" + str(par)]:
                      e.activation(sqscr, x, AF.Square, bias=0.0, scale=1.0,
                                   accum_out=a[:, 0:1]), sw)
            mslot = SLOT["tmp0"]
            v_op(lambda e, i=SLOT["xsum" + str(par)], o=mslot:
                 e.tensor_scalar_mul(o, i, 1.0 / D), [("s", sm)], inc=None)
            v_op(lambda e, i=SLOT["var" + str(par)], o=SLOT["tmp1"]:
                 e.tensor_scalar_mul(o, i, 1.0 / D), inc=None)
            vsync()
            v_op(lambda e, m=mslot, o=SLOT["tmp2"]: e.tensor_mul(o, m, m), inc=None)
            vsync()
            vv = v_op(lambda e, o=SLOT["var" + str(par)]:
                      e.tensor_sub(o, SLOT["tmp1"], SLOT["tmp2"]))
            l1 = s_op(lambda e, i=SLOT["var" + str(par)]:
                      e.activation(SLOT["lntmp"], i, AF.Ln, bias=epsc[:, 0:1],
                                   scale=1.0), [("v", vv)])
            e1 = s_op(lambda e, o=SLOT["rstd" + str(par)]:
                      e.activation(o, SLOT["lntmp"], AF.Exp, bias=0.0,
                                   scale=nhc[:, 0:1]), [("s", l1)])
            nm = v_op(lambda e, m=mslot, r=SLOT["rstd" + str(par)],
                      o=SLOT["mr" + str(par)]:
                      e.tensor_scalar(o, m, r[:, 0:1], -1.0, op0=ALU.mult,
                                      op1=ALU.mult), [("s", e1)])
            aw = [("v", nm)]
            if nt >= 2:
                aw.append(("t", trans_mark[nt - 2]))
            am = s_op(lambda e, x=x_t[par], o=u_tm[par], r=SLOT["rstd" + str(par)],
                      b=SLOT["mr" + str(par)]:
                      e.activation(o, x, AF.Identity, bias=b[:, 0:1], scale=r[:, 0:1]), aw)
            apply_mark[nt] = am
            maybe_ph0()
            for g in range(4):
                half = g % 2
                gw = [("s", am), ("wl", wl_consts)]
                prev = (nt, g - 2) if g >= 2 else (nt - 1, g + 2)
                if prev in evac_mark:
                    gw.append(("v", evac_mark[prev]))
                tm = None
                for j in range(4):
                    dc = 4 * g + j
                    tm = t_op(lambda e, i=u_tm[par][:, dc * 128:(dc + 1) * 128],
                              o=psTb[:, half * 1024 + j * 128: half * 1024 + (j + 1) * 128]:
                              e.transpose(o, i, ident[:]), gw if j == 0 else (),
                              inc="t" if j == 3 else None)
                trans_mark[nt] = tm
                em = v_op(lambda e, o=u3[:, 4 * g:4 * (g + 1), nt * 128:(nt + 1) * 128],
                          i=psTb[:, half * 1024: half * 1024 + 512].rearrange(
                              "p (a b) -> p a b", a=4):
                          e.tensor_copy(o, i), [("t", tm)])
                evac_mark[(nt, g)] = em
        while interleave_ph0 and ph0_i[0] < len(chunks):
            sched_ph0_chunk(ph0_i[0])
            ph0_i[0] += 1
        return evac_mark[(NT - 1, 3)]

    def sched_heads(seq, u_ready_v, extra_entry=()):
        entry = [("v", u_ready_v)] + list(extra_entry)
        cm1 = K.dma("sync", ctok[:], ctok_in, entry, inc="wl")
        K.dma("sync", ssg[:], ssg_in, inc="wl")
        tables_wl = C["wl"]

        wb_mark = {0: K.dma("sync", wbf[0][:], wq_scr[0],
                            entry + [("so", ph0_head_ready(0))], inc="wl")}
        prev_qkv_w_reader = {}
        num_evac_s = attn_last["s"]
        ao_mult_v = attn_last["v"]
        bcm = attn_last["t"]
        last_ao_v = 0

        ast_ = {"nem": num_evac_s, "aom": ao_mult_v, "bcm": bcm, "kti": 0,
                "er": 0, "rcm": 0}
        sq_mark = {}
        p_mark = {}
        p_eng = {}
        dn_mark = {}
        pv_mark = {}
        ropeg = {}
        vevg = {}
        sqmg = {}
        trmg = {}
        gbeg = {}
        pend_tr = [None]
        pend_qc = []
        ktq = []

        def drain(n):
            while n > 0 and ktq:
                fn, kt = ktq.pop(0)
                fn(kt)
                n -= 1

        for h in range(n_heads):
            par = h % 2
            if h + 1 < n_heads:
                w = [("so", ph0_head_ready(h + 1))]
                if h >= 1:
                    w.append(("t", prev_qkv_w_reader[h - 1]))
                else:
                    w += entry
                wb_mark[h + 1] = K.dma("sync", wbf[(h + 1) % 2][:], wq_scr[h + 1], w,
                                       inc="wl")

            it_marks = {}

            def emit_tr(pit, it_marks=it_marks, par=par, h=h):
                pnt = 2 * pit
                g = h * n_iters + pit
                halfT = pit % 2
                apm = it_marks[(pit, "apm")]
                trm = None
                for j in range(4):
                    w2 = [("v", apm)]
                    if g - 2 in gbeg:
                        w2.append(("s", gbeg[g - 2]))
                    trm = t_op(lambda e, i=tmps[:, j * 128:(j + 1) * 128],
                               o=psTb[:, halfT * 1024 + j * 128:
                                      halfT * 1024 + (j + 1) * 128]:
                               e.transpose(o, i, ident[:]), w2 if j == 0 else (),
                               inc="t" if j == 3 else None)
                it_marks[(pit, "trm")] = trm
                trmg[g] = trm
                s_op(lambda e, o=qfm[par][:, pnt * 128:(pnt + 2) * 128],
                     i=psTb[:, halfT * 1024: halfT * 1024 + 256]:
                     e.activation(o, i, AF.Identity, bias=bcol[:, 0:1],
                                  scale=gcol[:, 0:1]), [("t", trm)], inc=None)
                gb2 = s_op(lambda e, o=kfm[par][:, pnt * 128:(pnt + 2) * 128],
                           i=psTb[:, halfT * 1024 + 256: halfT * 1024 + 512]:
                           e.activation(o, i, AF.Identity, bias=bcol[:, 0:1],
                                        scale=gcol[:, 0:1]))
                it_marks[(pit, "gbe")] = gb2
                gbeg[g] = gb2

            def sched_qc(qc, it_marks=it_marks, par=par, h=h):
                qgate = it_marks[(2 * qc + 1, "gbe")]
                ktn = 4 * qc + 4
                qsl = slice(qc * 512, (qc + 1) * 512)

                def emit_pv(kt, kti, it_marks=it_marks, par=par, h=h):
                    wt = [(p_eng[kti], p_mark[kti]),
                          ("s", it_marks[(kt // 2, "vev")])]
                    if kt == 0:
                        wt.append(("s", ast_["nem"]))
                        wt.append(("v", ast_["aom"]))
                    vsrc = vx[par] if kt < 4 else vtm
                    pvm = t_op(lambda e, l=vsrc[:, kt * 128:(kt + 1) * 128],
                               r=p_b[kti % 2], o=psN[:],
                               st=(kt == 0), sp=(kt == ktn - 1):
                               e.matmul(o, l, r, start=st, stop=sp), wt)
                    dnm = t_op(lambda e, r=p_b[kti % 2], o=psD[0:1, :],
                               st=(kt == 0), sp=(kt == ktn - 1):
                               e.matmul(o, onesb[:], r, start=st, stop=sp))
                    pv_mark[kti] = pvm
                    dn_mark[kti] = dnm

                def unit(kt, qgate=qgate, ktn=ktn, qsl=qsl, qc=qc,
                         it_marks=it_marks, par=par, h=h, emit_pv=emit_pv):
                    kti = ast_["kti"]
                    w = [("s", qgate)]
                    if kti >= 2:
                        w.append(("s", sq_mark[kti - 2]))
                    smm = t_op(lambda e, l=kfm[par][:, kt * 128:(kt + 1) * 128],
                               o=psS[:, (kti % 2) * 512:(kti % 2) * 512 + 512],
                               r=qfm[par][:, qsl]:
                               e.matmul(o, l, r, start=True, stop=True), w)
                    if kt >= 1:
                        emit_pv(kt - 1, kti - 1)
                    ws = [("t", smm)]
                    if kti >= 2 and p_eng[kti - 2] == "v":
                        ws.append(("v", p_mark[kti - 2]))
                    if kt == 0 and ast_.get("rcm"):
                        ws.append(("v", ast_["rcm"]))
                    sqm2 = s_op(lambda e,
                                i=psS[:, (kti % 2) * 512:(kti % 2) * 512 + 512],
                                o=sq_b[kti % 2]:
                                e.activation(o, i, AF.Square, bias=0.0, scale=1.0),
                                ws)
                    sq_mark[kti] = sqm2
                    wv = [("s", sqm2)]
                    if kti >= 2:
                        wv.append(("t", dn_mark[kti - 2]))
                    if kt >= 4 * qc:
                        rel = kt * 128 - qc * 512
                        msl = maskw[:, 384 - rel: 896 - rel]
                        if kt == 4 * qc and ast_.get("er"):
                            wv.append(("s", ast_["er"]))
                        v_op(lambda e, i=sq_b[kti % 2], m=msl:
                             e.tensor_mul(tdiag, i, m), wv, inc=None)
                        pm = v_op(lambda e, i=sq_b[kti % 2], o=p_b[kti % 2]:
                                  e.tensor_mul(o, tdiag, i))
                        p_eng[kti] = "v"
                    else:
                        pm = s_op(lambda e, i=sq_b[kti % 2], o=p_b[kti % 2]:
                                  e.activation(o, i, AF.Square, bias=0.0,
                                               scale=1.0),
                                  [w2 for w2 in wv if w2[0] != "s"])
                        p_eng[kti] = "s"
                    p_mark[kti] = pm
                    ast_["kti"] = kti + 1
                    if kt < ktn - 1:
                        return
                    kti = ast_["kti"]
                    emit_pv(ktn - 1, kti - 1)
                    tsm = v_op(lambda e: e.tensor_scalar_mul(rd_b, psD[0:1, :],
                                                             DEN_SC),
                               [("t", dn_mark[kti - 1])])
                    bc1 = t_op(lambda e: e.matmul(psD[:], scvec[:], rd_b,
                                                  start=True, stop=True),
                               [("v", tsm)])
                    ast_["bcm"] = bc1
                    r0sb = sq_b[kti % 2]
                    lr = s_op(lambda e: e.activation(tdiag, psD[:], AF.Ln,
                                                     bias=0.0, scale=1.0),
                              [("t", bc1)])
                    er = s_op(lambda e, o=r0sb: e.activation(o, tdiag, AF.Exp,
                                                             bias=0.0, scale=-1.0),
                              [("s", lr)])
                    ast_["er"] = er
                    v_op(lambda e, i1=r0sb: e.scalar_tensor_tensor(
                        tdiag, psD[:], -1.0, i1, op0=ALU.mult, op1=ALU.mult),
                        [("s", er)], inc=None)
                    rcm = v_op(lambda e, i1=r0sb: e.scalar_tensor_tensor(
                        psD[:], tdiag, 2.0, i1, op0=ALU.add, op1=ALU.mult))
                    ast_["rcm"] = rcm
                    nem = s_op(lambda e: e.copy(num_sb, psN[:]),
                               [("t", pv_mark[kti - 1])])
                    ast_["nem"] = nem
                    ast_["aom"] = v_op(
                        lambda e,
                        o=ao[:, h * S + qc * 512: h * S + (qc + 1) * 512]:
                        e.tensor_tensor(o, num_sb, psD[:], op=ALU.mult),
                        [("s", nem)])

                for kt in range(ktn):
                    ktq.append((unit, kt))

            for it in range(n_iters):
                g = h * n_iters + it
                nt = 2 * it
                w = [("wl", wb_mark[h]), ("v", u_ready_v)]
                if g >= 1:
                    w.append(("v", ropeg[g - 1]))
                    w.append(("s", vevg[g - 1]))
                else:
                    w.append(("t", bcm))
                    w.append(("v", ao_mult_v))
                    w.append(("s", num_evac_s))
                mmm = None
                for sub in range(2):
                    base = 512 * sub
                    ntt = nt + sub
                    for dc in range(DC):
                        last = (sub == 1 and dc == DC - 1)
                        mmm = t_op(lambda e, l=u3[:, dc, ntt * 128:(ntt + 1) * 128],
                                   r=wbf[par][:, dc * 384:(dc + 1) * 384],
                                   o=psA[:, base:base + 384],
                                   st=(dc == 0), sp=(dc == DC - 1):
                                   e.matmul(o, l, r, start=st, stop=sp),
                                   w if (sub == 0 and dc == 0) else (),
                                   inc="t" if last else None)
                        if dc % 4 == 3:
                            drain(1)
                it_marks[(it, "qkv")] = mmm
                if head_stage >= 4 and pend_tr[0] is not None:
                    trfn, pit, qcfn = pend_tr[0]
                    pend_tr[0] = None
                    trfn(pit)
                    if do_attn and pit % 2 == 1:
                        pend_qc.append((qcfn, pit // 2, g))
                if head_stage >= 4 and pend_qc and pend_qc[0][2] < g:
                    qcfn2, j2, _ = pend_qc.pop(0)
                    qcfn2(j2)

                vw = [("t", mmm)]
                if h == 0 and it == 0:
                    vw += extra_entry
                vdst = vx[par][:, nt * 128:(nt + 2) * 128] if it < 2 else \
                    vtm[:, nt * 128:(nt + 2) * 128]
                vev = s_op(lambda e, o=vdst: e.copy(o, v_ps), vw)
                it_marks[(it, "vev")] = vev
                vevg[g] = vev

                qrap = qr
                if head_stage < 1:
                    it_marks[(it, "ropeV")] = C["v"]
                    it_marks[(it, "sqm")] = C["s"]
                    it_marks[(it, "trm")] = C["t"]
                    it_marks[(it, "gbe")] = C["s"]
                    ropeg[g] = C["v"]
                    sqmg[g] = C["s"]
                    trmg[g] = C["t"]
                    gbeg[g] = C["s"]
                    continue
                mw = [("t", mmm), ("wl", tables_wl), ("s", vev)]
                if g >= 1:
                    mw.append(("s", sqmg[g - 1]))
                    mw.append(("t", trmg[g - 1]))
                if h == 0 and it == 0:
                    mw += extra_entry
                mc = None
                for a_ in range(2):
                    for t_ in range(2):
                        qcol = a_ * 256 + t_ * 128
                        pcol = t_ * 512 + a_ * 128
                        scol = (nt + t_) * 128
                        mc = v_op(lambda e, o=tmpc[:, qcol:qcol + 128],
                                  i0=psA[:, pcol:pcol + 128],
                                  i1=ctok[:, scol:scol + 128]:
                                  e.tensor_tensor(o, i0, i1, op=ALU.mult),
                                  mw if (a_ == 0 and t_ == 0) else (), inc=None)
                ms2 = None
                for a_ in range(2):
                    for t_ in range(2):
                        qcol = a_ * 256 + t_ * 128
                        pcol = t_ * 512 + a_ * 128
                        scol = (nt + t_) * 128
                        ms2 = v_op(lambda e, o=tmps[:, qcol:qcol + 64],
                                   i0=psA[:, pcol + 64:pcol + 128],
                                   i1=ssg[:, scol:scol + 64]:
                                   e.tensor_tensor(o, i0, i1, op=ALU.mult),
                                   inc=None)
                        ms2 = v_op(lambda e, o=tmps[:, qcol + 64:qcol + 128],
                                   i0=psA[:, pcol:pcol + 64],
                                   i1=ssg[:, scol + 64:scol + 128]:
                                   e.tensor_tensor(o, i0, i1, op=ALU.mult),
                                   inc="v" if (a_ == 1 and t_ == 1) else None)
                it_marks[(it, "ropeV")] = ms2
                ropeg[g] = ms2
                if head_stage < 2:
                    it_marks[(it, "sqm")] = C["s"]
                    it_marks[(it, "trm")] = C["t"]
                    it_marks[(it, "gbe")] = C["s"]
                    sqmg[g] = C["s"]
                    trmg[g] = C["t"]
                    gbeg[g] = C["s"]
                    continue
                addm = None
                for j in range(4):
                    addm = v_op(lambda e, o=qrap[:, j * 128:(j + 1) * 128],
                                i0=tmps[:, j * 128:(j + 1) * 128],
                                i1=tmpc[:, j * 128:(j + 1) * 128]:
                                e.scalar_tensor_tensor(o, i0, 1.0, i1, op0=ALU.mult,
                                                       op1=ALU.add),
                                inc="v" if j == 3 else None)
                sqm = s_op(lambda e, i=qrap: e.activation(tmpc, i, AF.Square,
                                                          bias=0.0, scale=1.0),
                           [("v", addm)])
                qr3 = qrap.rearrange("p (g f) -> p g f", g=4)
                tc3 = tmpc.rearrange("p (g f) -> p g f", g=4)
                v_op(lambda e, i=qr3: e.reduce_sum(q4, i, axis=AXX), inc=None)
                v_op(lambda e, i=tc3: e.reduce_sum(qs4, i, axis=AXX),
                     [("s", sqm)], inc=None)
                vsync()
                if os.environ.get("KCOLL") and h == 0 and it == int(os.environ.get("KIT", "0")):
                    v_op(lambda e: e.tensor_copy(stat[:, 368:372], q4), inc=None)
                    v_op(lambda e: e.tensor_copy(stat[:, 372:376], qs4), inc=None)
                    s_op(lambda e, i=qrap: e.copy(tdiag, i), [("v", C["v"])], inc=None)
                v_op(lambda e: e.tensor_scalar_mul(m4, q4, 1.0 / HD), inc=None)
                v_op(lambda e: e.tensor_scalar_mul(v4, qs4, 1.0 / HD), inc=None)
                vsync()
                v_op(lambda e: e.tensor_mul(m4sq, m4, m4), inc=None)
                vsync()
                vv4 = v_op(lambda e: e.tensor_sub(v4, v4, m4sq))
                l4 = s_op(lambda e: e.activation(mr4, v4, AF.Ln, bias=epsc[:, 0:1],
                                                 scale=1.0), [("v", vv4)])
                em4 = s_op(lambda e: e.activation(rstd4, mr4, AF.Exp, bias=0.0,
                                                  scale=nhc[:, 0:1]), [("s", l4)])
                mrm = v_op(lambda e: e.tensor_mul(mr4, m4, rstd4), [("s", em4)])
                if os.environ.get("KCOLL") and h == 0 and it == int(os.environ.get("KIT", "0")):
                    v_op(lambda e: e.tensor_copy(stat[:, 376:380], v4), inc=None)
                    mrm = v_op(lambda e: e.tensor_copy(stat[:, 380:384], rstd4), [("v", mrm)])
                if head_stage < 3:
                    it_marks[(it, "sqm")] = sqm
                    it_marks[(it, "trm")] = C["t"]
                    it_marks[(it, "gbe")] = C["s"]
                    sqmg[g] = sqm
                    trmg[g] = C["t"]
                    gbeg[g] = C["s"]
                    continue
                K.emit("vector", lambda e: e.engine_nop(), [("v", mrm)], "vs", 1)
                apm = None
                for j in range(4):
                    apm = v_op(lambda e, o=tmps[:, j * 128:(j + 1) * 128],
                               i0=qrap[:, j * 128:(j + 1) * 128],
                               r=rstd4[:, j:j + 1], m=mr4[:, j:j + 1]:
                               e.tensor_scalar(o, i0, r, m, op0=ALU.mult,
                                               op1=ALU.subtract),
                               [("vs", C["vs"])] if j == 0 else (),
                               inc="v" if j == 3 else None)
                it_marks[(it, "sqm")] = sqm
                sqmg[g] = sqm
                it_marks[(it, "apm")] = apm
                if head_stage < 4:
                    it_marks[(it, "trm")] = C["t"]
                    it_marks[(it, "gbe")] = C["s"]
                    trmg[g] = C["t"]
                    gbeg[g] = C["s"]
                    continue
                pend_tr[0] = (emit_tr, it, sched_qc)
            prev_qkv_w_reader[h] = it_marks[(n_iters - 1, "qkv")]
            if not do_attn:
                if head_stage >= 4 and pend_tr[0] is not None:
                    trfn, pit, qcfn = pend_tr[0]
                    pend_tr[0] = None
                    trfn(pit)
                num_evac_s = it_marks[(n_iters - 1, "gbe")]
                ao_mult_v = C["v"]
                bcm = C["t"]
                last_ao_v = C["v"]
                continue

        if head_stage >= 4 and pend_tr[0] is not None:
            trfn, pit, qcfn = pend_tr[0]
            pend_tr[0] = None
            trfn(pit)
            if do_attn and pit % 2 == 1:
                pend_qc.append((qcfn, pit // 2, 0))
        while pend_qc:
            qcfn2, j2, _ = pend_qc.pop(0)
            qcfn2(j2)
        drain(len(ktq))

        last_qkv_mm[0] = prev_qkv_w_reader[n_heads - 1]
        if do_attn and head_stage >= 4 and n_heads:
            bcm = ast_["bcm"]
            ao_mult_v = ast_["aom"]
            num_evac_s = ast_["nem"]
            last_ao_v = ao_mult_v
        attn_last["t"] = bcm
        attn_last["v"] = ao_mult_v
        attn_last["s"] = num_evac_s
        return last_ao_v

    def sched_ph5(seq, ao_ready_v):
        pw = [("t", last_qkv_mm[0]), ("so", ph0_marks[len(chunks) - 1]),
              ("v", ao_ready_v)]
        K.dma("sync", uh, wo_scr.rearrange("h p f -> p h f"), pw, inc="wl")
        K.dma("sync", ffnA[:], fa_scr, inc="wl")
        K.dma("sync", wrA[:], wr_scr, inc="wl")
        K.dma("sync", ffnB[:], fb_scr, inc="wl")
        pro_wl = C["wl"]
        atl = attn_last["t"]

        r1_mark = {}
        bev_mark = {}
        zmul_mark = {}
        lg_mark = {}
        store_mark = {}
        wfm_evac = {}
        app_last = {}
        w_fm3 = w_fm.rearrange("p (dc f) -> p dc f", dc=DC)
        oc_bank = [psA[:, 0:512], psA[:, 512:1024], psS[:, 0:512], psS[:, 512:1024]]
        avst = {}
        avq = []

        def adrain(n):
            while n > 0 and avq:
                fn, hh = avq.pop(0)
                fn(hh)
                n -= 1

        def enq_av(nt):
            def chunk(hh, nt=nt):
                w = []
                if hh == 0:
                    w = [("wl", pro_wl), ("v", ao_ready_v)]
                    if nt >= 1:
                        for oc in range(OC):
                            w.append(("v", r1_mark[(nt - 1, oc)]))
                for oc in range(OC):
                    m = t_op(lambda e,
                             l=ao[:, hh * S + nt * 128: hh * S + (nt + 1) * 128],
                             r=uh[:, hh, oc * 512:(oc + 1) * 512],
                             o=oc_bank[oc], st=(hh == 0), sp=(hh == H - 1):
                             e.matmul(o, l, r, start=st, stop=sp),
                             w if oc == 0 else (),
                             inc="t" if hh == H - 1 else None)
                    if hh == H - 1:
                        avst[oc] = m
            for hh in range(H):
                avq.append((chunk, hh))

        xbuf = [xst, xstB]
        xa_mark = {}

        def issue_x(ntt, w=()):
            b = xbuf[ntt % 2]
            m0 = K.dma("gpsimd", b[0],
                       x_in[seq * S + ntt * 128: seq * S + (ntt + 1) * 128, 0:1024],
                       list(w), inc="xa")
            m1 = K.dma("gpsimd", b[1],
                       x_in[seq * S + ntt * 128: seq * S + (ntt + 1) * 128,
                            1024:2048], inc="xa")
            xa_mark[ntt] = (m0, m1)

        issue_x(0, [("t", atl), ("v", ao_ready_v)])
        issue_x(1, ())
        enq_av(0)
        for nt in range(NT):
            par = nt % 2
            xb = xbuf[nt % 2]
            xm0, xm1 = xa_mark[nt]
            adrain(len(avq))
            for oc in range(OC):
                xh = xb[oc // 2][:, (oc % 2) * 512:(oc % 2) * 512 + 512]
                rw = [("t", avst[oc]), ("xa", xm1 if oc >= 2 else xm0),
                      ("v", ao_ready_v)]
                if (nt - 1) in app_last and oc == 0:
                    rw.append(("s", app_last[nt - 1]))
                rm = v_op(lambda e, o=r1[:, oc * 512:(oc + 1) * 512], i0=oc_bank[oc],
                          i1=xh:
                          e.scalar_tensor_tensor(o, i0, 1.0, i1, op0=ALU.mult,
                                                 op1=ALU.add), rw)
                r1_mark[(nt, oc)] = rm
            if nt + 2 < NT:
                issue_x(nt + 2, [("v", r1_mark[(nt, 3)])])
            if nt + 1 < NT:
                enq_av(nt + 1)
            adrain(7)
            rm3 = r1_mark[(nt, 3)]
            xsum_s = SLOT["xsum" + str(par)]
            var_s = SLOT["var" + str(par)]
            rstd_s = SLOT["rstd" + str(par)]
            mr_s = SLOT["mr" + str(par)]
            vrm = v_op(lambda e, o=xsum_s: e.reduce_sum(o, r1, axis=AXX),
                       [("v", rm3)])
            sw = [("v", rm3), ("t", atl)]
            ssm = s_op(lambda e, a=var_s: e.activation(w_tm, r1, AF.Square, bias=0.0,
                                                       scale=1.0, accum_out=a[:, 0:1]),
                       sw)
            v_op(lambda e, i=xsum_s: e.tensor_scalar_mul(SLOT["tmp0"], i, 1.0 / D),
                 [("s", ssm), ("v", vrm)], inc=None)
            v_op(lambda e, i=var_s: e.tensor_scalar_mul(SLOT["tmp1"], i, 1.0 / D),
                 inc=None)
            vsync()
            v_op(lambda e: e.tensor_mul(SLOT["tmp2"], SLOT["tmp0"], SLOT["tmp0"]),
                 inc=None)
            vsync()
            vvm = v_op(lambda e, o=var_s: e.tensor_sub(o, SLOT["tmp1"], SLOT["tmp2"]))
            l2m = s_op(lambda e, i=var_s: e.activation(SLOT["lntmp"], i, AF.Ln,
                                                       bias=epsc[:, 0:1], scale=1.0),
                       [("v", vvm)])
            e2m = s_op(lambda e, o=rstd_s: e.activation(o, SLOT["lntmp"], AF.Exp,
                                                        bias=0.0, scale=nhc[:, 0:1]),
                       [("s", l2m)])
            nmm = v_op(lambda e, o=mr_s, r=rstd_s: e.tensor_scalar(
                o, SLOT["tmp0"], r[:, 0:1], -1.0, op0=ALU.mult, op1=ALU.mult),
                       [("s", e2m)])
            apm_g = {}
            for g in range(4):
                apm_g[g] = s_op(lambda e, b=mr_s, r=rstd_s,
                                o=w_tm[:, g * 512:(g + 1) * 512],
                                i=r1[:, g * 512:(g + 1) * 512]:
                                e.activation(o, i, AF.Identity, bias=b[:, 0:1],
                                             scale=r[:, 0:1]),
                                [("v", nmm)] if g == 0 else ())
            app_last[nt] = apm_g[3]
            for g in range(4):
                half = g % 2
                gw = [("s", apm_g[g])]
                prev = (nt, g - 2) if g >= 2 else (nt - 1, g + 2)
                if prev in wfm_evac:
                    gw.append(("s", wfm_evac[prev]))
                tm = None
                for j in range(4):
                    dc = 4 * g + j
                    tm = t_op(lambda e, i=w_tm[:, dc * 128:(dc + 1) * 128],
                              o=psTb[:, half * 1024 + j * 128: half * 1024 + (j + 1) * 128]:
                              e.transpose(o, i, ident[:]), gw if j == 0 else (),
                              inc="t" if j == 3 else None)
                adrain(1)
                em = s_op(lambda e, o=w_fm3[:, 4 * g:4 * (g + 1), :],
                          i=psTb[:, half * 1024: half * 1024 + 512].rearrange(
                              "p (a b) -> p a b", a=4):
                          e.copy(o, i), [("t", tm)])
                wfm_evac[(nt, g)] = em
            hw = [("s", wfm_evac[(nt, 0)])]
            if nt >= 1:
                hw.append(("v", zmul_mark[nt - 1]))
                hw.append(("v", lg_mark[nt - 1]))
                hw.append(("v", bev_mark[(nt - 1, 3)]))
            for dc in range(DC):
                hwd = hw if dc == 0 else (
                    [("s", wfm_evac[(nt, dc // 4)])] if dc % 4 == 0 else ())
                t_op(lambda e, l=w_fm[:, dc * 128:(dc + 1) * 128],
                     r=ffnA[:, dc * 512:(dc + 1) * 512], o=psN[:],
                     st=(dc == 0), sp=(dc == DC - 1):
                     e.matmul(o, l, r, start=st, stop=sp), hwd,
                     inc=None)
                t_op(lambda e, l=w_fm[:, dc * 128:(dc + 1) * 128],
                     r=wrA[:, dc * 8:(dc + 1) * 8], o=psD[:, 0:8],
                     st=(dc == 0), sp=(dc == DC - 1):
                     e.matmul(o, l, r, start=st, stop=sp),
                     inc="t" if dc == DC - 1 else None)
                if dc % 2 == 1:
                    adrain(1)
            lmm = C["t"]
            lgm = v_op(lambda e: e.tensor_tensor(logit_sb, psD[:, 0:8], brx[:],
                                                 op=ALU.add),
                       [("t", lmm), ("wl", wl_consts)])
            lg_mark[nt] = lgm
            vsync()
            ngm = v_op(lambda e: e.reduce_max(SLOT["negm"], logit_sb, axis=AXX,
                                              negate=True))
            exm = s_op(lambda e: e.activation(esb8, logit_sb, AF.Exp,
                                              bias=SLOT["negm"][:, 0:1], scale=1.0),
                       [("v", ngm)])
            v_op(lambda e: e.reduce_sum(SLOT["ssum"], esb8, axis=AXX),
                 [("s", exm)], inc=None)
            vsync()
            v_op(lambda e: e.tensor_scalar_mul(SLOT["w64"], SLOT["ssum"], float(RK)),
                 inc=None)
            vsync()
            v_op(lambda e: e.reciprocal(SLOT["w64"], SLOT["w64"]), inc=None)
            v_op(lambda e: e.tensor_scalar(mask8, logit_sb, SLOT["negm"][:, 0:1], 0.0,
                                           op0=ALU.add, op1=ALU.is_equal), inc=None)
            vsync()
            v_op(lambda e: e.tensor_copy(mask_exp.rearrange("p (a b) -> p a b", a=8),
                                         mask8.unsqueeze(2).broadcast_to([128, 8, 64])),
                 inc=None)
            zm = v_op(lambda e: e.tensor_tensor(z_tm, psN[:], mask_exp, op=ALU.mult))
            zmul_mark[nt] = zm
            adrain(2)
            ztm = None
            for j in range(ERC):
                zw = [("v", zm)]
                if zfm_s[0]:
                    zw.append(("s", zfm_s[0]))
                ztm = t_op(lambda e, i=z_tm[:, j * 128:(j + 1) * 128],
                           o=psTb[:, j * 128:(j + 1) * 128]:
                           e.transpose(o, i, ident[:]), zw if j == 0 else (),
                           inc="t" if j == ERC - 1 else None)
            zem = s_op(lambda e: e.copy(z_fm, psTb[:, 0:512]), [("t", ztm)])
            zfm_s[0] = zem
            bst = {}
            for oc in range(OC):
                bw = [("s", zem)] if oc == 0 else [("v", bev_mark[(nt, oc - 1)])]
                for erc in range(ERC):
                    m = t_op(lambda e, l=z_fm[:, erc * 128:(erc + 1) * 128],
                             r=ffnB[:, erc * 2048 + oc * 512:
                                     erc * 2048 + (oc + 1) * 512],
                             o=psN[:], st=(erc == 0), sp=(erc == ERC - 1):
                             e.matmul(o, l, r, start=st, stop=sp),
                             bw if erc == 0 else (),
                             inc="t" if erc == ERC - 1 else None)
                    if erc == ERC - 1:
                        bst[oc] = m
                adrain(1)
                idx = nt * 4 + oc
                ow = [("t", bst[oc])]
                if idx >= 2:
                    ow.append(("so", store_mark[idx - 2]))
                bm = v_op(lambda e, o=out_sb[oc % 2], i0=psN[:],
                          i1=r1[:, oc * 512:(oc + 1) * 512]:
                          e.scalar_tensor_tensor(o, i0, SLOT["w64"][:, 0:1], i1,
                                                 op0=ALU.mult, op1=ALU.add), ow)
                bev_mark[(nt, oc)] = bm
                store_mark[idx] = K.dma(
                    "gpsimd",
                    y_out[seq * S + nt * 128: seq * S + (nt + 1) * 128,
                          oc * 512:(oc + 1) * 512],
                    out_sb[oc % 2], [("v", bm)], inc="so")
            ph5_last["t"] = bst[OC - 1]
        return store_mark[NT * 4 - 1]

    dump = os.environ.get("KDUMP", "")

    def sched_dump(src_buf, gate):
        yb = y_out.bitcast(BF16)
        for c in range(8):
            K.dma("gpsimd", yb[c * 128:(c + 1) * 128, :],
                  src_buf[:, c * 4096:(c + 1) * 4096], [gate], inc="so")

    for seq in range(n_seqs):
        if seq == 0:
            u_ready = sched_ph1(0, interleave_ph0=True)
            ph0_all = [("so", ph0_marks[len(chunks) - 1])]
            if n_heads:
                ao_ready = sched_heads(0, u_ready, extra_entry=ph0_all)
        else:
            u_ready = sched_ph1(seq, interleave_ph0=False, entry_t=C["t"])
            if n_heads:
                ao_ready = sched_heads(seq, u_ready)
        if seq == 0 and dump == "u":
            sched_dump(u_wout[:], ("v", u_ready))
            yb2 = y_out.bitcast(BF16)
            K.dma("gpsimd", y_out[1500:1628, 0:32], stat[:, 0:32], [("v", u_ready)], inc="so")
            K.dma("gpsimd", yb2[1700:1828, 0:2048], u_tm[0][:], [("v", u_ready)], inc="so")
            K.dma("gpsimd", yb2[1900:2028, 0:2048], u_tm[1][:], [("v", u_ready)], inc="so")
            K.dma("gpsimd", y_out[2100:2228, :], x_t[0][:], [("v", u_ready)], inc="so")
            K.dma("gpsimd", y_out[2300:2428, :], x_t[1][:], [("v", u_ready)], inc="so")
            break
        if seq == 0 and dump == "ao" and n_heads:
            sched_dump(ao[:], ("v", ao_ready))
            break
        if seq == 0 and dump == "qk" and n_heads:
            g8 = ("s", C["s"])
            gv = ("v", C["v"])
            yb3 = y_out.bitcast(BF16)
            K.dma("gpsimd", yb3[0:128, 0:2048], qfm[0][:], [g8], inc="so")
            K.dma("gpsimd", yb3[200:328, 0:2048], kfm[0][:], [g8], inc="so")
            K.dma("gpsimd", yb3[400:528, 0:2048], vtm[:], [g8], inc="so")
            K.dma("gpsimd", yb3[600:728, 0:512], qr[:], [gv], inc="so")
            K.dma("gpsimd", yb3[800:928, 0:512], vx[0][:], [gv], inc="so")
            K.dma("gpsimd", yb3[1000:1128, 0:512], tmpc[:], [gv], inc="so")
            K.dma("gpsimd", y_out[1200:1328, 0:128], stat[:, 256:384], [gv], inc="so")
            K.dma("gpsimd", y_out[1400:1528, 0:512], tdiag, [gv], inc="so")
            break
        if n_heads and do_ph5:
            sched_ph5(seq, ao_ready)

    total_so = C["so"]
    K.emit("gpsimd", lambda e: e.engine_nop(), [("so", total_so)], None)

    with es:
        K.build()
    return nc


def _host_layouts(inputs):
    wqkv = np.ascontiguousarray(inputs["L0_attn_w_qkv"], dtype=np.float32)
    wout = np.ascontiguousarray(inputs["L0_attn_w_out"], dtype=np.float32)
    lns = np.ascontiguousarray(inputs["L0_attn_ln_scale"], dtype=np.float32)
    lnb = np.ascontiguousarray(inputs["L0_attn_ln_bias"], dtype=np.float32)
    A = np.ascontiguousarray(inputs["L0_ffn_A"], dtype=np.float32)
    Bw = np.ascontiguousarray(inputs["L0_ffn_B"], dtype=np.float32)
    Wr = np.ascontiguousarray(inputs["L0_ffn_Wr"], dtype=np.float32)
    br = np.ascontiguousarray(inputs["L0_ffn_br"], dtype=np.float32)

    d = {}
    d["wqkv"] = np.ascontiguousarray(
        wqkv.reshape(DC, 128, 3, H, 128).transpose(3, 0, 1, 2, 4).reshape(H, DC, 128, 384))
    d["wout"] = np.ascontiguousarray(wout.reshape(H, 128, D))
    d["ffnA"] = np.ascontiguousarray(
        A.transpose(2, 0, 1).reshape(D, ER).reshape(DC, 128, ER).transpose(1, 0, 2)
        .reshape(128, DC * ER))
    d["wr"] = np.ascontiguousarray(
        Wr.T.reshape(DC, 128, NE).transpose(1, 0, 2).reshape(128, DC * NE))
    d["ffnB"] = np.ascontiguousarray(
        Bw.transpose(0, 2, 1).reshape(ER, D).reshape(ERC, 128, D).transpose(1, 0, 2)
        .reshape(128, ERC * D))
    d["ident"] = np.eye(128, dtype=ml_dtypes.bfloat16)
    inv = 1.0 / (10000.0 ** (np.arange(0, HD, 2, dtype=np.float64) / HD))
    t = np.arange(S, dtype=np.float64)[:, None] * inv[None, :]
    cosv = np.cos(t).astype(np.float32)
    sinv = np.sin(t).astype(np.float32)
    Cfull = np.concatenate([cosv, cosv], axis=1)
    Ssg = np.concatenate([-sinv, sinv], axis=1)
    d["ctok"] = np.ascontiguousarray(
        Cfull.reshape(NT, 128, HD).transpose(1, 0, 2).reshape(128, NT * HD))
    d["ssg"] = np.ascontiguousarray(
        Ssg.reshape(NT, 128, HD).transpose(1, 0, 2).reshape(128, NT * HD))
    mw = (np.arange(896, dtype=np.int64)[None, :] >=
          (np.arange(128, dtype=np.int64)[:, None] + 384)).astype(ml_dtypes.bfloat16)
    d["maskw"] = np.ascontiguousarray(mw)
    d["gcolv"] = np.ascontiguousarray(lns.reshape(128, 1))
    d["bcolv"] = np.ascontiguousarray(lnb.reshape(128, 1))
    d["brx"] = np.ascontiguousarray(np.broadcast_to(br[None, :], (128, 8)))
    d["onesb"] = np.ones((128, 1), dtype=ml_dtypes.bfloat16)
    d["epsc"] = np.full((128, 1), EPS, dtype=np.float32)
    d["nhc"] = np.full((128, 1), -0.5, dtype=np.float32)
    d["scv"] = np.full((1, 128), 1.0, dtype=np.float32)
    return d


def kernel(**inputs):
    x = np.ascontiguousarray(inputs["L0_attn_x"], dtype=np.float32)
    shared = _host_layouts(inputs)
    nc = build_program()
    in_maps = []
    for c in range(NC):
        m = dict(shared)
        m["x"] = np.ascontiguousarray(x[2 * c: 2 * c + 2].reshape(SEQS * S, D))
        in_maps.append(m)
    res = run_bass_kernel_spmd(nc, in_maps, core_ids=list(range(NC)))
    out = np.concatenate([r["y"].reshape(SEQS, S, D) for r in res.results], axis=0)
    return out


# version 22
# speedup vs baseline: 1.0208x; 1.0031x over previous
import os
import numpy as np
import ml_dtypes
import concourse.bass as bass
import concourse.mybir as mybir
from concourse.bass_utils import run_bass_kernel_spmd

F32 = mybir.dt.float32
BF16 = mybir.dt.bfloat16
U8 = mybir.dt.uint8
AF = mybir.ActivationFunctionType
ALU = mybir.AluOpType
AXX = mybir.AxisListType.X

NC = 8
SEQS = 2
S = 2048
D = 2048
H = 16
HD = 128
NT = 16
DC = 16
EPS = 1e-5
NE, RK = 8, 64
ER = NE * RK
ERC = 4
OC = 4
DEN_SC = 1.0

ENGINES = ("sync", "scalar", "vector", "tensor", "gpsimd")


class KB:
    def __init__(self, nc):
        self.nc = nc
        self.ops = {e: [] for e in ENGINES}
        self.sems = {}
        self.cnt = {}
        self._ctx = []

    def sem(self, name):
        if name not in self.sems:
            cm = self.nc.semaphore(name)
            self.sems[name] = cm.__enter__()
            self._ctx.append(cm)
            self.cnt[name] = 0
        return name

    def emit(self, eng, fn, waits=(), inc=None, inc_n=1):
        ws = tuple((s, v) for (s, v) in waits if v > 0)
        self.ops[eng].append((fn, ws, inc, inc_n))
        if inc is not None:
            self.cnt[inc] += inc_n
            return self.cnt[inc]
        return None

    def dma(self, eng, out, in_, waits=(), inc=None):
        def fn(e, out=out, in_=in_):
            return e.dma_start(out=out, in_=in_)
        return self.emit(eng, fn, waits, inc, 16)

    def build(self):
        with self.nc.Block() as blk:
            for eng in ENGINES:
                getattr(blk, eng)(self._mk(eng))

    def _mk(self, eng):
        ops = self.ops[eng]
        sems = self.sems

        def body(e):
            seen = {}
            for fn, waits, inc, inc_n in ops:
                for s_, v_ in waits:
                    if seen.get(s_, 0) < v_:
                        e.wait_ge(sems[s_], v_)
                        seen[s_] = v_
                inst = fn(e)
                if inc is not None:
                    inst.then_inc(sems[inc], inc_n)
        return body


def build_program(n_heads=H, do_ph5=True, n_seqs=SEQS, do_attn=True, head_stage=4, n_iters=8):
    nc = bass.Bass("TRN2", target_bir_lowering=False, detect_race_conditions=False)

    x_in = nc.dram_tensor("x", [SEQS * S, D], F32, kind="ExternalInput").ap()
    y_out = nc.dram_tensor("y", [SEQS * S, D], F32, kind="ExternalOutput").ap()
    wqkv_in = nc.dram_tensor("wqkv", [H, DC, 128, 384], F32, kind="ExternalInput").ap()
    wout_in = nc.dram_tensor("wout", [H, 128, D], F32, kind="ExternalInput").ap()
    ffnA_in = nc.dram_tensor("ffnA", [128, DC * ER], F32, kind="ExternalInput").ap()
    wr_in = nc.dram_tensor("wr", [128, DC * NE], F32, kind="ExternalInput").ap()
    ffnB_in = nc.dram_tensor("ffnB", [128, ERC * D], F32, kind="ExternalInput").ap()
    ident_in = nc.dram_tensor("ident", [128, 128], BF16, kind="ExternalInput").ap()
    ctok_in = nc.dram_tensor("ctok", [128, NT * HD], F32, kind="ExternalInput").ap()
    ssg_in = nc.dram_tensor("ssg", [128, NT * HD], F32, kind="ExternalInput").ap()
    maskw_in = nc.dram_tensor("maskw", [128, 896], BF16, kind="ExternalInput").ap()
    gcol_in = nc.dram_tensor("gcolv", [128, 1], F32, kind="ExternalInput").ap()
    bcol_in = nc.dram_tensor("bcolv", [128, 1], F32, kind="ExternalInput").ap()
    br_in = nc.dram_tensor("brx", [128, 8], F32, kind="ExternalInput").ap()
    onesb_in = nc.dram_tensor("onesb", [128, 1], BF16, kind="ExternalInput").ap()
    eps_in = nc.dram_tensor("epsc", [128, 1], F32, kind="ExternalInput").ap()
    nh_in = nc.dram_tensor("nhc", [128, 1], F32, kind="ExternalInput").ap()
    scv_in = nc.dram_tensor("scv", [1, 128], F32, kind="ExternalInput").ap()

    wq_scr = nc.dram_tensor("wq_scr", [H, 128, DC * 384], BF16, kind="Internal").ap()
    wo_scr = nc.dram_tensor("wo_scr", [H, 128, D], BF16, kind="Internal").ap()
    fa_scr = nc.dram_tensor("fa_scr", [128, DC * ER], BF16, kind="Internal").ap()
    wr_scr = nc.dram_tensor("wr_scr", [128, DC * NE], BF16, kind="Internal").ap()
    fb_scr = nc.dram_tensor("fb_scr", [128, ERC * D], BF16, kind="Internal").ap()

    import contextlib
    es = contextlib.ExitStack()

    def sbuf(name, shape, dt):
        return es.enter_context(nc.sbuf_tensor(name, shape, dt))

    u_wout = sbuf("u_wout", [128, DC * S], BF16)
    ao = sbuf("ao", [128, H * S], BF16)
    ident = sbuf("identb", [128, 128], BF16)
    gcol = sbuf("gcol", [128, 1], F32)
    bcol = sbuf("bcol", [128, 1], F32)
    onesb = sbuf("onesbf", [128, 1], BF16)
    scvec = sbuf("scvec", [1, 128], F32)
    maskw = sbuf("maskwb", [128, 896], BF16)
    brx = sbuf("brxb", [128, 8], F32)
    epsc = sbuf("epscb", [128, 1], F32)
    nhc = sbuf("nhcb", [128, 1], F32)
    stat = sbuf("stat", [128, 384], F32)
    arena = sbuf("arena", [128, 76800], U8)

    psA = es.enter_context(nc.psum_tensor("psA", [128, 1024], F32))
    psT = es.enter_context(nc.psum_tensor("psT", [128, 1024], F32))
    psS = es.enter_context(nc.psum_tensor("psS", [128, 1024], F32))
    psN = es.enter_context(nc.psum_tensor("psN", [128, 512], F32))
    psD = es.enter_context(nc.psum_tensor("psD", [128, 512], F32))
    psTb = psT[:].bitcast(BF16)

    def av(off, nbytes, dt):
        return arena[:, off:off + nbytes].bitcast(dt)

    x_t = [av(0, 8192, F32), av(8192, 8192, F32)]
    sqscr = av(16384, 8192, F32)
    u_tm = [av(33792, 4096, BF16), av(37888, 4096, BF16)]
    wstage = [av(43008, 6144, F32), av(49152, 6144, F32)]
    wcvt = [av(55296, 3072, BF16), av(58368, 3072, BF16)]

    wbf = [av(0, 12288, BF16), av(12288, 12288, BF16)]
    ctok = av(24576, 8192, F32)
    ssg = av(32768, 8192, F32)
    qfm = [av(40960, 4096, BF16), av(68608, 4096, BF16)]
    kfm = [av(45056, 4096, BF16), av(72704, 4096, BF16)]
    vtm = av(49152, 4096, BF16)
    vx = [av(54272, 1024, BF16), av(56320, 1024, BF16)]
    qr = av(53248, 1024, BF16)
    tmpc = av(55296, 1024, BF16)
    p_b = [av(57344, 1024, BF16), av(58368, 1024, BF16)]
    tmps = av(63488, 1024, BF16)
    sq_b = [av(64512, 2048, F32), av(66560, 2048, F32)]
    tdiag = av(59392, 2048, F32)
    rd_full = av(59392, 2048, F32)
    rd_b = rd_full[0:1, :]
    num_sb = av(61440, 1024, BF16)

    ffnA = av(0, 16384, BF16)
    wrA = av(16384, 256, BF16)
    ffnB = av(16640, 16384, BF16)
    r1 = av(33024, 8192, F32)
    xst = [av(41216, 4096, F32), av(45312, 4096, F32)]
    w_tm = av(49408, 4096, BF16)
    w_fm = av(53504, 4096, BF16)
    z_tm = av(57600, 1024, BF16)
    z_fm = av(58624, 1024, BF16)
    logit_sb = av(59648, 32, F32)
    mask8 = av(59680, 16, BF16)
    mask_exp = av(59712, 1024, BF16)
    out_sb = [av(68608, 2048, F32), av(74752, 2048, F32)]
    xstB = [av(60736, 4096, F32), av(70656, 4096, F32)]

    SLOT = {}
    for i, nm in enumerate(["xsum0", "xsum1", "var0", "var1", "rstd0", "rstd1",
                            "mr0", "mr1", "lntmp", "negm", "ssum", "w64",
                            "tmp0", "tmp1", "tmp2", "tmp3"]):
        SLOT[nm] = stat[:, i * 16:i * 16 + 1]
    q4 = stat[:, 256:260]
    qs4 = stat[:, 272:276]
    m4 = stat[:, 288:292]
    v4 = stat[:, 304:308]
    rstd4 = stat[:, 320:324]
    mr4 = stat[:, 336:340]
    esb8 = stat[:, 352:360]
    m4sq = stat[:, 360:364]

    K = KB(nc)
    for s_ in ["t", "v", "s", "xa", "wl", "so", "sq", "vs"]:
        K.sem(s_)
    C = K.cnt

    def t_op(fn, waits=(), inc="t"):
        return K.emit("tensor", fn, waits, inc)

    def v_op(fn, waits=(), inc="v"):
        return K.emit("vector", fn, waits, inc)

    def s_op(fn, waits=(), inc="s"):
        return K.emit("scalar", fn, waits, inc)

    def vsync():
        m = K.emit("vector", lambda e: e.engine_nop(), (), "vs", 1)
        K.emit("vector", lambda e: e.engine_nop(), [("vs", m)], None)

    u3 = u_wout[:].rearrange("p (dc s) -> p dc s", dc=DC)
    uh = u_wout[:].rearrange("p (h f) -> p h f", h=H)
    ct3 = ctok.rearrange("p (nt f) -> p nt f", nt=NT)
    sg3 = ssg.rearrange("p (nt f) -> p nt f", nt=NT)
    psA4 = psA[:].rearrange("p (t g f) -> p g t f", t=2, g=8)
    psA4 = psA[:].rearrange("p (t g f) -> p g t f", t=2, g=4)
    qk_ps = psA4[:, 0:2]
    v_ps = psA4[:, 2]

    chunks = []
    for h in range(H):
        for g in range(4):
            chunks.append((wqkv_in[h, 4 * g:4 * (g + 1)].rearrange("c p f -> p c f"),
                           wq_scr[h][:, g * 1536:(g + 1) * 1536], 1536))
    for h in range(H):
        for g in range(2):
            chunks.append((wout_in[h][:, g * 1024:(g + 1) * 1024],
                           wo_scr[h][:, g * 1024:(g + 1) * 1024], 1024))
    for g in range(8):
        chunks.append((ffnA_in[:, g * 1024:(g + 1) * 1024],
                       fa_scr[:, g * 1024:(g + 1) * 1024], 1024))
    chunks.append((wr_in[:, 0:128], wr_scr[:, 0:128], 128))
    for g in range(8):
        chunks.append((ffnB_in[:, g * 1024:(g + 1) * 1024],
                       fb_scr[:, g * 1024:(g + 1) * 1024], 1024))

    K.dma("sync", ident[:], ident_in, inc="wl")
    K.dma("sync", gcol[:], gcol_in, inc="wl")
    K.dma("sync", bcol[:], bcol_in, inc="wl")
    K.dma("sync", onesb[:], onesb_in, inc="wl")
    K.dma("sync", scvec[:], scv_in, inc="wl")
    K.dma("sync", maskw[:], maskw_in, inc="wl")
    K.dma("sync", brx[:], br_in, inc="wl")
    K.dma("sync", epsc[:], eps_in, inc="wl")
    K.dma("sync", nhc[:], nh_in, inc="wl")
    wl_consts = C["wl"]

    ph0_marks = {}
    ph0_cv = {}

    def sched_ph0_chunk(i):
        src, dst, ncols = chunks[i]
        par = i % 2
        stg = wstage[par][:, 0:ncols]
        cvt = wcvt[par][:, 0:ncols]
        waits = []
        if i >= 2:
            waits.append(ph0_cv[i - 2])
        ldm = K.dma("sync", stg, src, waits, inc="wl")
        cwaits = [("wl", ldm)]
        if i >= 2:
            cwaits.append(("so", ph0_marks[i - 2]))
        if i % 2 == 0:
            cm = v_op(lambda e, o=cvt, s2=stg: e.tensor_copy(o, s2), cwaits)
            ph0_cv[i] = ("v", cm)
        else:
            cm = s_op(lambda e, o=cvt, s2=stg: e.copy(o, s2), cwaits)
            ph0_cv[i] = ("s", cm)
        ph0_marks[i] = K.dma("gpsimd", dst, cvt, [ph0_cv[i]], inc="so")

    def ph0_head_ready(h):
        return ph0_marks[4 * h + 3]

    last_qkv_mm = [0]
    attn_last = {"t": 0, "v": 0, "s": 0}
    ph5_last = {"t": 0, "v": 0, "s": 0, "so": 0}
    zfm_s = [0]

    def sched_ph1(seq, interleave_ph0, entry_t=0):
        ph0_i = [0]

        def maybe_ph0(k=2):
            for _ in range(k):
                if interleave_ph0 and ph0_i[0] < len(chunks):
                    sched_ph0_chunk(ph0_i[0])
                    ph0_i[0] += 1

        apply_mark = {}
        trans_mark = {}
        evac_mark = {}
        red_mark = {}
        for nt in range(NT):
            par = nt % 2
            maybe_ph0()
            waits = [("t", entry_t)] if nt < 2 else \
                [("s", apply_mark[nt - 2]), ("v", red_mark[nt - 2])]
            xm = K.dma("gpsimd", x_t[par],
                       x_in[seq * S + nt * 128: seq * S + (nt + 1) * 128, :], waits, inc="xa")
            rm = v_op(lambda e, x=x_t[par], o=SLOT["xsum" + str(par)]:
                      e.reduce_sum(o, x, axis=AXX), [("xa", xm)])
            red_mark[nt] = rm
            sw = [("xa", xm)]
            if nt < 2 and entry_t:
                sw.append(("t", entry_t))
            sm = s_op(lambda e, x=x_t[par], a=SLOT["var" + str(par)]:
                      e.activation(sqscr, x, AF.Square, bias=0.0, scale=1.0,
                                   accum_out=a[:, 0:1]), sw)
            mslot = SLOT["tmp0"]
            v_op(lambda e, i=SLOT["xsum" + str(par)], o=mslot:
                 e.tensor_scalar_mul(o, i, 1.0 / D), [("s", sm)], inc=None)
            v_op(lambda e, i=SLOT["var" + str(par)], o=SLOT["tmp1"]:
                 e.tensor_scalar_mul(o, i, 1.0 / D), inc=None)
            vsync()
            v_op(lambda e, m=mslot, o=SLOT["tmp2"]: e.tensor_mul(o, m, m), inc=None)
            vsync()
            vv = v_op(lambda e, o=SLOT["var" + str(par)]:
                      e.tensor_sub(o, SLOT["tmp1"], SLOT["tmp2"]))
            l1 = s_op(lambda e, i=SLOT["var" + str(par)]:
                      e.activation(SLOT["lntmp"], i, AF.Ln, bias=epsc[:, 0:1],
                                   scale=1.0), [("v", vv)])
            e1 = s_op(lambda e, o=SLOT["rstd" + str(par)]:
                      e.activation(o, SLOT["lntmp"], AF.Exp, bias=0.0,
                                   scale=nhc[:, 0:1]), [("s", l1)])
            nm = v_op(lambda e, m=mslot, r=SLOT["rstd" + str(par)],
                      o=SLOT["mr" + str(par)]:
                      e.tensor_scalar(o, m, r[:, 0:1], -1.0, op0=ALU.mult,
                                      op1=ALU.mult), [("s", e1)])
            aw = [("v", nm)]
            if nt >= 2:
                aw.append(("t", trans_mark[nt - 2]))
            am = s_op(lambda e, x=x_t[par], o=u_tm[par], r=SLOT["rstd" + str(par)],
                      b=SLOT["mr" + str(par)]:
                      e.activation(o, x, AF.Identity, bias=b[:, 0:1], scale=r[:, 0:1]), aw)
            apply_mark[nt] = am
            maybe_ph0()
            for g in range(4):
                half = g % 2
                gw = [("s", am), ("wl", wl_consts)]
                prev = (nt, g - 2) if g >= 2 else (nt - 1, g + 2)
                if prev in evac_mark:
                    gw.append(("v", evac_mark[prev]))
                tm = None
                for j in range(4):
                    dc = 4 * g + j
                    tm = t_op(lambda e, i=u_tm[par][:, dc * 128:(dc + 1) * 128],
                              o=psTb[:, half * 1024 + j * 128: half * 1024 + (j + 1) * 128]:
                              e.transpose(o, i, ident[:]), gw if j == 0 else (),
                              inc="t" if j == 3 else None)
                trans_mark[nt] = tm
                em = v_op(lambda e, o=u3[:, 4 * g:4 * (g + 1), nt * 128:(nt + 1) * 128],
                          i=psTb[:, half * 1024: half * 1024 + 512].rearrange(
                              "p (a b) -> p a b", a=4):
                          e.tensor_copy(o, i), [("t", tm)])
                evac_mark[(nt, g)] = em
        while interleave_ph0 and ph0_i[0] < len(chunks):
            sched_ph0_chunk(ph0_i[0])
            ph0_i[0] += 1
        return evac_mark[(NT - 1, 3)]

    def sched_heads(seq, u_ready_v, extra_entry=()):
        entry = [("v", u_ready_v)] + list(extra_entry)
        cm1 = K.dma("sync", ctok[:], ctok_in, entry, inc="wl")
        K.dma("sync", ssg[:], ssg_in, inc="wl")
        tables_wl = C["wl"]

        wb_mark = {0: K.dma("sync", wbf[0][:], wq_scr[0],
                            entry + [("so", ph0_head_ready(0))], inc="wl")}
        prev_qkv_w_reader = {}
        num_evac_s = attn_last["s"]
        ao_mult_v = attn_last["v"]
        bcm = attn_last["t"]
        last_ao_v = 0

        ast_ = {"nem": num_evac_s, "aom": ao_mult_v, "bcm": bcm, "kti": 0,
                "er": 0, "rcm": 0}
        sq_mark = {}
        p_mark = {}
        p_eng = {}
        dn_mark = {}
        pv_mark = {}
        ropeg = {}
        vevg = {}
        sqmg = {}
        trmg = {}
        gbeg = {}
        pend_tr = [None]
        pend_qc = []
        ktq = []

        def drain(n):
            while n > 0 and ktq:
                fn, kt = ktq.pop(0)
                fn(kt)
                n -= 1

        for h in range(n_heads):
            par = h % 2
            if h + 1 < n_heads:
                w = [("so", ph0_head_ready(h + 1))]
                if h >= 1:
                    w.append(("t", prev_qkv_w_reader[h - 1]))
                else:
                    w += entry
                wb_mark[h + 1] = K.dma("sync", wbf[(h + 1) % 2][:], wq_scr[h + 1], w,
                                       inc="wl")

            it_marks = {}

            def emit_tr(pit, it_marks=it_marks, par=par, h=h):
                pnt = 2 * pit
                g = h * n_iters + pit
                halfT = pit % 2
                apm = it_marks[(pit, "apm")]
                trm = None
                for j in range(4):
                    w2 = [("v", apm)]
                    if g - 2 in gbeg:
                        w2.append(("s", gbeg[g - 2]))
                    trm = t_op(lambda e, i=tmps[:, j * 128:(j + 1) * 128],
                               o=psTb[:, halfT * 1024 + j * 128:
                                      halfT * 1024 + (j + 1) * 128]:
                               e.transpose(o, i, ident[:]), w2 if j == 0 else (),
                               inc="t" if j == 3 else None)
                it_marks[(pit, "trm")] = trm
                trmg[g] = trm
                s_op(lambda e, o=qfm[par][:, pnt * 128:(pnt + 2) * 128],
                     i=psTb[:, halfT * 1024: halfT * 1024 + 256]:
                     e.activation(o, i, AF.Identity, bias=bcol[:, 0:1],
                                  scale=gcol[:, 0:1]), [("t", trm)], inc=None)
                gb2 = s_op(lambda e, o=kfm[par][:, pnt * 128:(pnt + 2) * 128],
                           i=psTb[:, halfT * 1024 + 256: halfT * 1024 + 512]:
                           e.activation(o, i, AF.Identity, bias=bcol[:, 0:1],
                                        scale=gcol[:, 0:1]))
                it_marks[(pit, "gbe")] = gb2
                gbeg[g] = gb2

            def sched_qc(qc, it_marks=it_marks, par=par, h=h):
                qgate = it_marks[(2 * qc + 1, "gbe")]
                ktn = 4 * qc + 4
                qsl = slice(qc * 512, (qc + 1) * 512)

                def emit_pv(kt, kti, it_marks=it_marks, par=par, h=h):
                    wt = [(p_eng[kti], p_mark[kti]),
                          ("s", it_marks[(kt // 2, "vev")])]
                    if kt == 0:
                        wt.append(("s", ast_["nem"]))
                        wt.append(("v", ast_["aom"]))
                    vsrc = vx[par] if kt < 4 else vtm
                    pvm = t_op(lambda e, l=vsrc[:, kt * 128:(kt + 1) * 128],
                               r=p_b[kti % 2], o=psN[:],
                               st=(kt == 0), sp=(kt == ktn - 1):
                               e.matmul(o, l, r, start=st, stop=sp), wt)
                    dnm = t_op(lambda e, r=p_b[kti % 2], o=psD[0:1, :],
                               st=(kt == 0), sp=(kt == ktn - 1):
                               e.matmul(o, onesb[:], r, start=st, stop=sp))
                    pv_mark[kti] = pvm
                    dn_mark[kti] = dnm

                def unit(kt, qgate=qgate, ktn=ktn, qsl=qsl, qc=qc,
                         it_marks=it_marks, par=par, h=h, emit_pv=emit_pv):
                    kti = ast_["kti"]
                    w = [("s", qgate)]
                    if kti >= 2:
                        w.append(("s", sq_mark[kti - 2]))
                    smm = t_op(lambda e, l=kfm[par][:, kt * 128:(kt + 1) * 128],
                               o=psS[:, (kti % 2) * 512:(kti % 2) * 512 + 512],
                               r=qfm[par][:, qsl]:
                               e.matmul(o, l, r, start=True, stop=True), w)
                    if kt >= 1:
                        emit_pv(kt - 1, kti - 1)
                    ws = [("t", smm)]
                    if kti >= 2 and p_eng[kti - 2] == "v":
                        ws.append(("v", p_mark[kti - 2]))
                    if kt == 0 and ast_.get("rcm"):
                        ws.append(("v", ast_["rcm"]))
                    sqm2 = s_op(lambda e,
                                i=psS[:, (kti % 2) * 512:(kti % 2) * 512 + 512],
                                o=sq_b[kti % 2]:
                                e.activation(o, i, AF.Square, bias=0.0, scale=1.0),
                                ws)
                    sq_mark[kti] = sqm2
                    wv = [("s", sqm2)]
                    if kti >= 2:
                        wv.append(("t", dn_mark[kti - 2]))
                    if kt >= 4 * qc:
                        rel = kt * 128 - qc * 512
                        msl = maskw[:, 384 - rel: 896 - rel]
                        if kt == 4 * qc and ast_.get("er"):
                            wv.append(("s", ast_["er"]))
                        v_op(lambda e, i=sq_b[kti % 2], m=msl:
                             e.tensor_mul(tdiag, i, m), wv, inc=None)
                        pm = v_op(lambda e, i=sq_b[kti % 2], o=p_b[kti % 2]:
                                  e.tensor_mul(o, tdiag, i))
                        p_eng[kti] = "v"
                    else:
                        pm = s_op(lambda e, i=sq_b[kti % 2], o=p_b[kti % 2]:
                                  e.activation(o, i, AF.Square, bias=0.0,
                                               scale=1.0),
                                  [w2 for w2 in wv if w2[0] != "s"])
                        p_eng[kti] = "s"
                    p_mark[kti] = pm
                    ast_["kti"] = kti + 1
                    if kt < ktn - 1:
                        return
                    kti = ast_["kti"]
                    emit_pv(ktn - 1, kti - 1)
                    tsm = v_op(lambda e: e.tensor_scalar_mul(rd_b, psD[0:1, :],
                                                             DEN_SC),
                               [("t", dn_mark[kti - 1])])
                    bc1 = t_op(lambda e: e.matmul(psD[:], scvec[:], rd_b,
                                                  start=True, stop=True),
                               [("v", tsm)])
                    ast_["bcm"] = bc1
                    r0sb = sq_b[kti % 2]
                    lr = s_op(lambda e: e.activation(tdiag, psD[:], AF.Ln,
                                                     bias=0.0, scale=1.0),
                              [("t", bc1)])
                    er = s_op(lambda e, o=r0sb: e.activation(o, tdiag, AF.Exp,
                                                             bias=0.0, scale=-1.0),
                              [("s", lr)])
                    ast_["er"] = er
                    v_op(lambda e, i1=r0sb: e.scalar_tensor_tensor(
                        tdiag, psD[:], -1.0, i1, op0=ALU.mult, op1=ALU.mult),
                        [("s", er)], inc=None)
                    rcm = v_op(lambda e, i1=r0sb: e.scalar_tensor_tensor(
                        psD[:], tdiag, 2.0, i1, op0=ALU.add, op1=ALU.mult))
                    ast_["rcm"] = rcm
                    nem = s_op(lambda e: e.copy(num_sb, psN[:]),
                               [("t", pv_mark[kti - 1])])
                    ast_["nem"] = nem
                    ast_["aom"] = v_op(
                        lambda e,
                        o=ao[:, h * S + qc * 512: h * S + (qc + 1) * 512]:
                        e.tensor_tensor(o, num_sb, psD[:], op=ALU.mult),
                        [("s", nem)])

                for kt in range(ktn):
                    ktq.append((unit, kt))

            for it in range(n_iters):
                g = h * n_iters + it
                nt = 2 * it
                w = [("wl", wb_mark[h]), ("v", u_ready_v)]
                if g >= 1:
                    w.append(("v", ropeg[g - 1]))
                    w.append(("s", vevg[g - 1]))
                else:
                    w.append(("t", bcm))
                    w.append(("v", ao_mult_v))
                    w.append(("s", num_evac_s))
                mmm = None
                for sub in range(2):
                    base = 512 * sub
                    ntt = nt + sub
                    for dc in range(DC):
                        last = (sub == 1 and dc == DC - 1)
                        mmm = t_op(lambda e, l=u3[:, dc, ntt * 128:(ntt + 1) * 128],
                                   r=wbf[par][:, dc * 384:(dc + 1) * 384],
                                   o=psA[:, base:base + 384],
                                   st=(dc == 0), sp=(dc == DC - 1):
                                   e.matmul(o, l, r, start=st, stop=sp),
                                   w if (sub == 0 and dc == 0) else (),
                                   inc="t" if last else None)
                        if dc % 2 == 1:
                            drain(1)
                it_marks[(it, "qkv")] = mmm
                if head_stage >= 4 and pend_tr[0] is not None:
                    trfn, pit, qcfn = pend_tr[0]
                    pend_tr[0] = None
                    trfn(pit)
                    if do_attn and pit % 2 == 1:
                        pend_qc.append((qcfn, pit // 2, g))
                if head_stage >= 4 and pend_qc and pend_qc[0][2] < g:
                    qcfn2, j2, _ = pend_qc.pop(0)
                    qcfn2(j2)

                vw = [("t", mmm)]
                if h == 0 and it == 0:
                    vw += extra_entry
                vdst = vx[par][:, nt * 128:(nt + 2) * 128] if it < 2 else \
                    vtm[:, nt * 128:(nt + 2) * 128]
                vev = s_op(lambda e, o=vdst: e.copy(o, v_ps), vw)
                it_marks[(it, "vev")] = vev
                vevg[g] = vev

                qrap = qr
                if head_stage < 1:
                    it_marks[(it, "ropeV")] = C["v"]
                    it_marks[(it, "sqm")] = C["s"]
                    it_marks[(it, "trm")] = C["t"]
                    it_marks[(it, "gbe")] = C["s"]
                    ropeg[g] = C["v"]
                    sqmg[g] = C["s"]
                    trmg[g] = C["t"]
                    gbeg[g] = C["s"]
                    continue
                mw = [("t", mmm), ("wl", tables_wl), ("s", vev)]
                if g >= 1:
                    mw.append(("s", sqmg[g - 1]))
                    mw.append(("t", trmg[g - 1]))
                if h == 0 and it == 0:
                    mw += extra_entry
                mc = None
                for a_ in range(2):
                    for t_ in range(2):
                        qcol = a_ * 256 + t_ * 128
                        pcol = t_ * 512 + a_ * 128
                        scol = (nt + t_) * 128
                        mc = v_op(lambda e, o=tmpc[:, qcol:qcol + 128],
                                  i0=psA[:, pcol:pcol + 128],
                                  i1=ctok[:, scol:scol + 128]:
                                  e.tensor_tensor(o, i0, i1, op=ALU.mult),
                                  mw if (a_ == 0 and t_ == 0) else (), inc=None)
                ms2 = None
                for a_ in range(2):
                    for t_ in range(2):
                        qcol = a_ * 256 + t_ * 128
                        pcol = t_ * 512 + a_ * 128
                        scol = (nt + t_) * 128
                        ms2 = v_op(lambda e, o=tmps[:, qcol:qcol + 64],
                                   i0=psA[:, pcol + 64:pcol + 128],
                                   i1=ssg[:, scol:scol + 64]:
                                   e.tensor_tensor(o, i0, i1, op=ALU.mult),
                                   inc=None)
                        ms2 = v_op(lambda e, o=tmps[:, qcol + 64:qcol + 128],
                                   i0=psA[:, pcol:pcol + 64],
                                   i1=ssg[:, scol + 64:scol + 128]:
                                   e.tensor_tensor(o, i0, i1, op=ALU.mult),
                                   inc="v" if (a_ == 1 and t_ == 1) else None)
                it_marks[(it, "ropeV")] = ms2
                ropeg[g] = ms2
                if head_stage < 2:
                    it_marks[(it, "sqm")] = C["s"]
                    it_marks[(it, "trm")] = C["t"]
                    it_marks[(it, "gbe")] = C["s"]
                    sqmg[g] = C["s"]
                    trmg[g] = C["t"]
                    gbeg[g] = C["s"]
                    continue
                addm = None
                for j in range(4):
                    addm = v_op(lambda e, o=qrap[:, j * 128:(j + 1) * 128],
                                i0=tmps[:, j * 128:(j + 1) * 128],
                                i1=tmpc[:, j * 128:(j + 1) * 128]:
                                e.scalar_tensor_tensor(o, i0, 1.0, i1, op0=ALU.mult,
                                                       op1=ALU.add),
                                inc="v" if j == 3 else None)
                sqm = s_op(lambda e, i=qrap: e.activation(tmpc, i, AF.Square,
                                                          bias=0.0, scale=1.0),
                           [("v", addm)])
                qr3 = qrap.rearrange("p (g f) -> p g f", g=4)
                tc3 = tmpc.rearrange("p (g f) -> p g f", g=4)
                v_op(lambda e, i=qr3: e.reduce_sum(q4, i, axis=AXX), inc=None)
                v_op(lambda e, i=tc3: e.reduce_sum(qs4, i, axis=AXX),
                     [("s", sqm)], inc=None)
                vsync()
                if os.environ.get("KCOLL") and h == 0 and it == int(os.environ.get("KIT", "0")):
                    v_op(lambda e: e.tensor_copy(stat[:, 368:372], q4), inc=None)
                    v_op(lambda e: e.tensor_copy(stat[:, 372:376], qs4), inc=None)
                    s_op(lambda e, i=qrap: e.copy(tdiag, i), [("v", C["v"])], inc=None)
                v_op(lambda e: e.tensor_scalar_mul(m4, q4, 1.0 / HD), inc=None)
                v_op(lambda e: e.tensor_scalar_mul(v4, qs4, 1.0 / HD), inc=None)
                vsync()
                v_op(lambda e: e.tensor_mul(m4sq, m4, m4), inc=None)
                vsync()
                vv4 = v_op(lambda e: e.tensor_sub(v4, v4, m4sq))
                l4 = s_op(lambda e: e.activation(mr4, v4, AF.Ln, bias=epsc[:, 0:1],
                                                 scale=1.0), [("v", vv4)])
                em4 = s_op(lambda e: e.activation(rstd4, mr4, AF.Exp, bias=0.0,
                                                  scale=nhc[:, 0:1]), [("s", l4)])
                mrm = v_op(lambda e: e.tensor_mul(mr4, m4, rstd4), [("s", em4)])
                if os.environ.get("KCOLL") and h == 0 and it == int(os.environ.get("KIT", "0")):
                    v_op(lambda e: e.tensor_copy(stat[:, 376:380], v4), inc=None)
                    mrm = v_op(lambda e: e.tensor_copy(stat[:, 380:384], rstd4), [("v", mrm)])
                if head_stage < 3:
                    it_marks[(it, "sqm")] = sqm
                    it_marks[(it, "trm")] = C["t"]
                    it_marks[(it, "gbe")] = C["s"]
                    sqmg[g] = sqm
                    trmg[g] = C["t"]
                    gbeg[g] = C["s"]
                    continue
                K.emit("vector", lambda e: e.engine_nop(), [("v", mrm)], "vs", 1)
                apm = None
                for j in range(4):
                    apm = v_op(lambda e, o=tmps[:, j * 128:(j + 1) * 128],
                               i0=qrap[:, j * 128:(j + 1) * 128],
                               r=rstd4[:, j:j + 1], m=mr4[:, j:j + 1]:
                               e.tensor_scalar(o, i0, r, m, op0=ALU.mult,
                                               op1=ALU.subtract),
                               [("vs", C["vs"])] if j == 0 else (),
                               inc="v" if j == 3 else None)
                it_marks[(it, "sqm")] = sqm
                sqmg[g] = sqm
                it_marks[(it, "apm")] = apm
                if head_stage < 4:
                    it_marks[(it, "trm")] = C["t"]
                    it_marks[(it, "gbe")] = C["s"]
                    trmg[g] = C["t"]
                    gbeg[g] = C["s"]
                    continue
                pend_tr[0] = (emit_tr, it, sched_qc)
            prev_qkv_w_reader[h] = it_marks[(n_iters - 1, "qkv")]
            if not do_attn:
                if head_stage >= 4 and pend_tr[0] is not None:
                    trfn, pit, qcfn = pend_tr[0]
                    pend_tr[0] = None
                    trfn(pit)
                num_evac_s = it_marks[(n_iters - 1, "gbe")]
                ao_mult_v = C["v"]
                bcm = C["t"]
                last_ao_v = C["v"]
                continue

        if head_stage >= 4 and pend_tr[0] is not None:
            trfn, pit, qcfn = pend_tr[0]
            pend_tr[0] = None
            trfn(pit)
            if do_attn and pit % 2 == 1:
                pend_qc.append((qcfn, pit // 2, 0))
        while pend_qc:
            qcfn2, j2, _ = pend_qc.pop(0)
            qcfn2(j2)
        drain(len(ktq))

        last_qkv_mm[0] = prev_qkv_w_reader[n_heads - 1]
        if do_attn and head_stage >= 4 and n_heads:
            bcm = ast_["bcm"]
            ao_mult_v = ast_["aom"]
            num_evac_s = ast_["nem"]
            last_ao_v = ao_mult_v
        attn_last["t"] = bcm
        attn_last["v"] = ao_mult_v
        attn_last["s"] = num_evac_s
        return last_ao_v

    def sched_ph5(seq, ao_ready_v):
        pw = [("t", last_qkv_mm[0]), ("so", ph0_marks[len(chunks) - 1]),
              ("v", ao_ready_v)]
        K.dma("sync", uh, wo_scr.rearrange("h p f -> p h f"), pw, inc="wl")
        K.dma("sync", ffnA[:], fa_scr, inc="wl")
        K.dma("sync", wrA[:], wr_scr, inc="wl")
        K.dma("sync", ffnB[:], fb_scr, inc="wl")
        pro_wl = C["wl"]
        atl = attn_last["t"]

        r1_mark = {}
        bev_mark = {}
        zmul_mark = {}
        lg_mark = {}
        store_mark = {}
        wfm_evac = {}
        app_last = {}
        w_fm3 = w_fm.rearrange("p (dc f) -> p dc f", dc=DC)
        oc_bank = [psA[:, 0:512], psA[:, 512:1024], psS[:, 0:512], psS[:, 512:1024]]
        avst = {}
        avq = []

        def adrain(n):
            while n > 0 and avq:
                fn, hh = avq.pop(0)
                fn(hh)
                n -= 1

        def enq_av(nt):
            def chunk(hh, nt=nt):
                w = []
                if hh == 0:
                    w = [("wl", pro_wl), ("v", ao_ready_v)]
                    if nt >= 1:
                        for oc in range(OC):
                            w.append(("v", r1_mark[(nt - 1, oc)]))
                for oc in range(OC):
                    m = t_op(lambda e,
                             l=ao[:, hh * S + nt * 128: hh * S + (nt + 1) * 128],
                             r=uh[:, hh, oc * 512:(oc + 1) * 512],
                             o=oc_bank[oc], st=(hh == 0), sp=(hh == H - 1):
                             e.matmul(o, l, r, start=st, stop=sp),
                             w if oc == 0 else (),
                             inc="t" if hh == H - 1 else None)
                    if hh == H - 1:
                        avst[oc] = m
            for hh in range(H):
                avq.append((chunk, hh))

        xbuf = [xst, xstB]
        xa_mark = {}

        def issue_x(ntt, w=()):
            b = xbuf[ntt % 2]
            m0 = K.dma("gpsimd", b[0],
                       x_in[seq * S + ntt * 128: seq * S + (ntt + 1) * 128, 0:1024],
                       list(w), inc="xa")
            m1 = K.dma("gpsimd", b[1],
                       x_in[seq * S + ntt * 128: seq * S + (ntt + 1) * 128,
                            1024:2048], inc="xa")
            xa_mark[ntt] = (m0, m1)

        issue_x(0, [("t", atl), ("v", ao_ready_v)])
        issue_x(1, ())
        enq_av(0)
        for nt in range(NT):
            par = nt % 2
            xb = xbuf[nt % 2]
            xm0, xm1 = xa_mark[nt]
            adrain(len(avq))
            for oc in range(OC):
                xh = xb[oc // 2][:, (oc % 2) * 512:(oc % 2) * 512 + 512]
                rw = [("t", avst[oc]), ("xa", xm1 if oc >= 2 else xm0),
                      ("v", ao_ready_v)]
                if (nt - 1) in app_last and oc == 0:
                    rw.append(("s", app_last[nt - 1]))
                rm = v_op(lambda e, o=r1[:, oc * 512:(oc + 1) * 512], i0=oc_bank[oc],
                          i1=xh:
                          e.scalar_tensor_tensor(o, i0, 1.0, i1, op0=ALU.mult,
                                                 op1=ALU.add), rw)
                r1_mark[(nt, oc)] = rm
            if nt + 2 < NT:
                issue_x(nt + 2, [("v", r1_mark[(nt, 3)])])
            if nt + 1 < NT:
                enq_av(nt + 1)
            adrain(7)
            rm3 = r1_mark[(nt, 3)]
            xsum_s = SLOT["xsum" + str(par)]
            var_s = SLOT["var" + str(par)]
            rstd_s = SLOT["rstd" + str(par)]
            mr_s = SLOT["mr" + str(par)]
            vrm = v_op(lambda e, o=xsum_s: e.reduce_sum(o, r1, axis=AXX),
                       [("v", rm3)])
            sw = [("v", rm3), ("t", atl)]
            ssm = s_op(lambda e, a=var_s: e.activation(w_tm, r1, AF.Square, bias=0.0,
                                                       scale=1.0, accum_out=a[:, 0:1]),
                       sw)
            v_op(lambda e, i=xsum_s: e.tensor_scalar_mul(SLOT["tmp0"], i, 1.0 / D),
                 [("s", ssm), ("v", vrm)], inc=None)
            v_op(lambda e, i=var_s: e.tensor_scalar_mul(SLOT["tmp1"], i, 1.0 / D),
                 inc=None)
            vsync()
            v_op(lambda e: e.tensor_mul(SLOT["tmp2"], SLOT["tmp0"], SLOT["tmp0"]),
                 inc=None)
            vsync()
            vvm = v_op(lambda e, o=var_s: e.tensor_sub(o, SLOT["tmp1"], SLOT["tmp2"]))
            l2m = s_op(lambda e, i=var_s: e.activation(SLOT["lntmp"], i, AF.Ln,
                                                       bias=epsc[:, 0:1], scale=1.0),
                       [("v", vvm)])
            e2m = s_op(lambda e, o=rstd_s: e.activation(o, SLOT["lntmp"], AF.Exp,
                                                        bias=0.0, scale=nhc[:, 0:1]),
                       [("s", l2m)])
            nmm = v_op(lambda e, o=mr_s, r=rstd_s: e.tensor_scalar(
                o, SLOT["tmp0"], r[:, 0:1], -1.0, op0=ALU.mult, op1=ALU.mult),
                       [("s", e2m)])
            apm_g = {}
            for g in range(4):
                apm_g[g] = s_op(lambda e, b=mr_s, r=rstd_s,
                                o=w_tm[:, g * 512:(g + 1) * 512],
                                i=r1[:, g * 512:(g + 1) * 512]:
                                e.activation(o, i, AF.Identity, bias=b[:, 0:1],
                                             scale=r[:, 0:1]),
                                [("v", nmm)] if g == 0 else ())
            app_last[nt] = apm_g[3]
            for g in range(4):
                half = g % 2
                gw = [("s", apm_g[g])]
                prev = (nt, g - 2) if g >= 2 else (nt - 1, g + 2)
                if prev in wfm_evac:
                    gw.append(("v", wfm_evac[prev]))
                tm = None
                for j in range(4):
                    dc = 4 * g + j
                    tm = t_op(lambda e, i=w_tm[:, dc * 128:(dc + 1) * 128],
                              o=psTb[:, half * 1024 + j * 128: half * 1024 + (j + 1) * 128]:
                              e.transpose(o, i, ident[:]), gw if j == 0 else (),
                              inc="t" if j == 3 else None)
                adrain(1)
                em = v_op(lambda e, o=w_fm3[:, 4 * g:4 * (g + 1), :],
                          i=psTb[:, half * 1024: half * 1024 + 512].rearrange(
                              "p (a b) -> p a b", a=4):
                          e.tensor_copy(o, i), [("t", tm)])
                wfm_evac[(nt, g)] = em
            hw = [("v", wfm_evac[(nt, 0)])]
            if nt >= 1:
                hw.append(("v", zmul_mark[nt - 1]))
                hw.append(("v", lg_mark[nt - 1]))
                hw.append(("v", bev_mark[(nt - 1, 3)]))
            for dc in range(DC):
                hwd = hw if dc == 0 else (
                    [("v", wfm_evac[(nt, dc // 4)])] if dc % 4 == 0 else ())
                t_op(lambda e, l=w_fm[:, dc * 128:(dc + 1) * 128],
                     r=ffnA[:, dc * 512:(dc + 1) * 512], o=psN[:],
                     st=(dc == 0), sp=(dc == DC - 1):
                     e.matmul(o, l, r, start=st, stop=sp), hwd,
                     inc=None)
                t_op(lambda e, l=w_fm[:, dc * 128:(dc + 1) * 128],
                     r=wrA[:, dc * 8:(dc + 1) * 8], o=psD[:, 0:8],
                     st=(dc == 0), sp=(dc == DC - 1):
                     e.matmul(o, l, r, start=st, stop=sp),
                     inc="t" if dc == DC - 1 else None)
                if dc % 2 == 1:
                    adrain(1)
            lmm = C["t"]
            lgm = v_op(lambda e: e.tensor_tensor(logit_sb, psD[:, 0:8], brx[:],
                                                 op=ALU.add),
                       [("t", lmm), ("wl", wl_consts)])
            lg_mark[nt] = lgm
            vsync()
            ngm = v_op(lambda e: e.reduce_max(SLOT["negm"], logit_sb, axis=AXX,
                                              negate=True))
            exm = s_op(lambda e: e.activation(esb8, logit_sb, AF.Exp,
                                              bias=SLOT["negm"][:, 0:1], scale=1.0),
                       [("v", ngm)])
            v_op(lambda e: e.reduce_sum(SLOT["ssum"], esb8, axis=AXX),
                 [("s", exm)], inc=None)
            vsync()
            v_op(lambda e: e.tensor_scalar_mul(SLOT["w64"], SLOT["ssum"], float(RK)),
                 inc=None)
            vsync()
            v_op(lambda e: e.reciprocal(SLOT["w64"], SLOT["w64"]), inc=None)
            v_op(lambda e: e.tensor_scalar(mask8, logit_sb, SLOT["negm"][:, 0:1], 0.0,
                                           op0=ALU.add, op1=ALU.is_equal), inc=None)
            vsync()
            v_op(lambda e: e.tensor_copy(mask_exp.rearrange("p (a b) -> p a b", a=8),
                                         mask8.unsqueeze(2).broadcast_to([128, 8, 64])),
                 inc=None)
            zm = v_op(lambda e: e.tensor_tensor(z_tm, psN[:], mask_exp, op=ALU.mult))
            zmul_mark[nt] = zm
            adrain(2)
            ztm = None
            for j in range(ERC):
                zw = [("v", zm)]
                if zfm_s[0]:
                    zw.append(("s", zfm_s[0]))
                ztm = t_op(lambda e, i=z_tm[:, j * 128:(j + 1) * 128],
                           o=psTb[:, j * 128:(j + 1) * 128]:
                           e.transpose(o, i, ident[:]), zw if j == 0 else (),
                           inc="t" if j == ERC - 1 else None)
            zem = s_op(lambda e: e.copy(z_fm, psTb[:, 0:512]), [("t", ztm)])
            zfm_s[0] = zem
            bst = {}
            for oc in range(OC):
                bw = [("s", zem)] if oc == 0 else [("v", bev_mark[(nt, oc - 1)])]
                for erc in range(ERC):
                    m = t_op(lambda e, l=z_fm[:, erc * 128:(erc + 1) * 128],
                             r=ffnB[:, erc * 2048 + oc * 512:
                                     erc * 2048 + (oc + 1) * 512],
                             o=psN[:], st=(erc == 0), sp=(erc == ERC - 1):
                             e.matmul(o, l, r, start=st, stop=sp),
                             bw if erc == 0 else (),
                             inc="t" if erc == ERC - 1 else None)
                    if erc == ERC - 1:
                        bst[oc] = m
                adrain(1)
                idx = nt * 4 + oc
                ow = [("t", bst[oc])]
                if idx >= 2:
                    ow.append(("so", store_mark[idx - 2]))
                bm = v_op(lambda e, o=out_sb[oc % 2], i0=psN[:],
                          i1=r1[:, oc * 512:(oc + 1) * 512]:
                          e.scalar_tensor_tensor(o, i0, SLOT["w64"][:, 0:1], i1,
                                                 op0=ALU.mult, op1=ALU.add), ow)
                bev_mark[(nt, oc)] = bm
                store_mark[idx] = K.dma(
                    "gpsimd",
                    y_out[seq * S + nt * 128: seq * S + (nt + 1) * 128,
                          oc * 512:(oc + 1) * 512],
                    out_sb[oc % 2], [("v", bm)], inc="so")
            ph5_last["t"] = bst[OC - 1]
        return store_mark[NT * 4 - 1]

    dump = os.environ.get("KDUMP", "")

    def sched_dump(src_buf, gate):
        yb = y_out.bitcast(BF16)
        for c in range(8):
            K.dma("gpsimd", yb[c * 128:(c + 1) * 128, :],
                  src_buf[:, c * 4096:(c + 1) * 4096], [gate], inc="so")

    for seq in range(n_seqs):
        if seq == 0:
            u_ready = sched_ph1(0, interleave_ph0=True)
            ph0_all = [("so", ph0_marks[len(chunks) - 1])]
            if n_heads:
                ao_ready = sched_heads(0, u_ready, extra_entry=ph0_all)
        else:
            u_ready = sched_ph1(seq, interleave_ph0=False, entry_t=C["t"])
            if n_heads:
                ao_ready = sched_heads(seq, u_ready)
        if seq == 0 and dump == "u":
            sched_dump(u_wout[:], ("v", u_ready))
            yb2 = y_out.bitcast(BF16)
            K.dma("gpsimd", y_out[1500:1628, 0:32], stat[:, 0:32], [("v", u_ready)], inc="so")
            K.dma("gpsimd", yb2[1700:1828, 0:2048], u_tm[0][:], [("v", u_ready)], inc="so")
            K.dma("gpsimd", yb2[1900:2028, 0:2048], u_tm[1][:], [("v", u_ready)], inc="so")
            K.dma("gpsimd", y_out[2100:2228, :], x_t[0][:], [("v", u_ready)], inc="so")
            K.dma("gpsimd", y_out[2300:2428, :], x_t[1][:], [("v", u_ready)], inc="so")
            break
        if seq == 0 and dump == "ao" and n_heads:
            sched_dump(ao[:], ("v", ao_ready))
            break
        if seq == 0 and dump == "qk" and n_heads:
            g8 = ("s", C["s"])
            gv = ("v", C["v"])
            yb3 = y_out.bitcast(BF16)
            K.dma("gpsimd", yb3[0:128, 0:2048], qfm[0][:], [g8], inc="so")
            K.dma("gpsimd", yb3[200:328, 0:2048], kfm[0][:], [g8], inc="so")
            K.dma("gpsimd", yb3[400:528, 0:2048], vtm[:], [g8], inc="so")
            K.dma("gpsimd", yb3[600:728, 0:512], qr[:], [gv], inc="so")
            K.dma("gpsimd", yb3[800:928, 0:512], vx[0][:], [gv], inc="so")
            K.dma("gpsimd", yb3[1000:1128, 0:512], tmpc[:], [gv], inc="so")
            K.dma("gpsimd", y_out[1200:1328, 0:128], stat[:, 256:384], [gv], inc="so")
            K.dma("gpsimd", y_out[1400:1528, 0:512], tdiag, [gv], inc="so")
            break
        if n_heads and do_ph5:
            sched_ph5(seq, ao_ready)

    total_so = C["so"]
    K.emit("gpsimd", lambda e: e.engine_nop(), [("so", total_so)], None)

    with es:
        K.build()
    return nc


def _host_layouts(inputs):
    wqkv = np.ascontiguousarray(inputs["L0_attn_w_qkv"], dtype=np.float32)
    wout = np.ascontiguousarray(inputs["L0_attn_w_out"], dtype=np.float32)
    lns = np.ascontiguousarray(inputs["L0_attn_ln_scale"], dtype=np.float32)
    lnb = np.ascontiguousarray(inputs["L0_attn_ln_bias"], dtype=np.float32)
    A = np.ascontiguousarray(inputs["L0_ffn_A"], dtype=np.float32)
    Bw = np.ascontiguousarray(inputs["L0_ffn_B"], dtype=np.float32)
    Wr = np.ascontiguousarray(inputs["L0_ffn_Wr"], dtype=np.float32)
    br = np.ascontiguousarray(inputs["L0_ffn_br"], dtype=np.float32)

    d = {}
    d["wqkv"] = np.ascontiguousarray(
        wqkv.reshape(DC, 128, 3, H, 128).transpose(3, 0, 1, 2, 4).reshape(H, DC, 128, 384))
    d["wout"] = np.ascontiguousarray(wout.reshape(H, 128, D))
    d["ffnA"] = np.ascontiguousarray(
        A.transpose(2, 0, 1).reshape(D, ER).reshape(DC, 128, ER).transpose(1, 0, 2)
        .reshape(128, DC * ER))
    d["wr"] = np.ascontiguousarray(
        Wr.T.reshape(DC, 128, NE).transpose(1, 0, 2).reshape(128, DC * NE))
    d["ffnB"] = np.ascontiguousarray(
        Bw.transpose(0, 2, 1).reshape(ER, D).reshape(ERC, 128, D).transpose(1, 0, 2)
        .reshape(128, ERC * D))
    d["ident"] = np.eye(128, dtype=ml_dtypes.bfloat16)
    inv = 1.0 / (10000.0 ** (np.arange(0, HD, 2, dtype=np.float64) / HD))
    t = np.arange(S, dtype=np.float64)[:, None] * inv[None, :]
    cosv = np.cos(t).astype(np.float32)
    sinv = np.sin(t).astype(np.float32)
    Cfull = np.concatenate([cosv, cosv], axis=1)
    Ssg = np.concatenate([-sinv, sinv], axis=1)
    d["ctok"] = np.ascontiguousarray(
        Cfull.reshape(NT, 128, HD).transpose(1, 0, 2).reshape(128, NT * HD))
    d["ssg"] = np.ascontiguousarray(
        Ssg.reshape(NT, 128, HD).transpose(1, 0, 2).reshape(128, NT * HD))
    mw = (np.arange(896, dtype=np.int64)[None, :] >=
          (np.arange(128, dtype=np.int64)[:, None] + 384)).astype(ml_dtypes.bfloat16)
    d["maskw"] = np.ascontiguousarray(mw)
    d["gcolv"] = np.ascontiguousarray(lns.reshape(128, 1))
    d["bcolv"] = np.ascontiguousarray(lnb.reshape(128, 1))
    d["brx"] = np.ascontiguousarray(np.broadcast_to(br[None, :], (128, 8)))
    d["onesb"] = np.ones((128, 1), dtype=ml_dtypes.bfloat16)
    d["epsc"] = np.full((128, 1), EPS, dtype=np.float32)
    d["nhc"] = np.full((128, 1), -0.5, dtype=np.float32)
    d["scv"] = np.full((1, 128), 1.0, dtype=np.float32)
    return d


def kernel(**inputs):
    x = np.ascontiguousarray(inputs["L0_attn_x"], dtype=np.float32)
    shared = _host_layouts(inputs)
    nc = build_program()
    in_maps = []
    for c in range(NC):
        m = dict(shared)
        m["x"] = np.ascontiguousarray(x[2 * c: 2 * c + 2].reshape(SEQS * S, D))
        in_maps.append(m)
    res = run_bass_kernel_spmd(nc, in_maps, core_ids=list(range(NC)))
    out = np.concatenate([r["y"].reshape(SEQS, S, D) for r in res.results], axis=0)
    return out
